# version 08
# baseline (speedup 1.0000x reference)
import numpy as np
import ml_dtypes
from contextlib import ExitStack

import concourse.bass as bass
import concourse.mybir as mybir
from concourse.bass_utils import run_bass_kernel_spmd

F32 = mybir.dt.float32
BF16 = mybir.dt.bfloat16
F16 = mybir.dt.float16
ALU = mybir.AluOpType
ACT = mybir.ActivationFunctionType
AX = mybir.AxisListType

H = 4096
NC = 8
P = 128


def _groups(n, g):
    base, rem = divmod(n, g)
    sizes = [base + (1 if i < rem else 0) for i in range(g)]
    starts = [int(x) for x in np.cumsum([0] + sizes[:-1])]
    return list(zip(starts, sizes))


class Ctr:
    def __init__(self, st, nc, name):
        self.h = st.enter_context(nc.semaphore(name))
        self.n = 0

    def inc(self, instr, k=1):
        instr.then_inc(self.h, k)
        self.n += k
        return self.n

    def wait(self, eng, v):
        assert 0 <= v <= self.n, f"{self.h}: wait on future count {v} > {self.n}"
        if v > 0:
            eng.wait_ge(self.h, v)


class SerEng:

    COMPUTE = frozenset({
        "memset", "memzero", "tensor_copy", "tensor_tensor", "tensor_add",
        "tensor_sub", "tensor_mul", "tensor_max", "tensor_scalar",
        "tensor_single_scalar", "tensor_scalar_add", "tensor_scalar_sub",
        "tensor_scalar_mul", "tensor_scalar_max", "tensor_scalar_min",
        "tensor_relu", "tensor_reduce", "reduce_max", "reduce_sum",
        "scalar_tensor_tensor", "reciprocal", "transpose", "bn_stats",
        "bn_aggr", "activation", "copy", "add", "mul", "square", "sqrt",
        "sign", "select",
    })

    def __init__(self, eng, priv):
        self._e = eng
        self._p = priv
        self._last = None

    def __getattr__(self, name):
        attr = getattr(self._e, name)
        if name not in SerEng.COMPUTE:
            return attr

        def shim(*a, _ctr=None, **kw):
            if self._last is not None:
                c, v = self._last
                c.wait(self._e, v)
            ins = attr(*a, **kw)
            c = _ctr if _ctr is not None else self._p
            v = c.inc(ins)
            self._last = (c, v)
            return v
        return shim


def build_nc(I=11008, T=1024, NG=8, NGD=8, phase_limit='full', dbg=False):
    KT = H // P
    MT = I // P
    TT = T // P
    TG = 4 if TT >= 8 else TT
    NTG = TT // TG
    HQ = H // 512
    NH = T // 512
    ggrp = _groups(MT, NG)
    dgrp = _groups(MT, NGD)
    NTOT = float(I) * float(H)

    nc = bass.Bass("TRN2", target_bir_lowering=False, debug=False,
                   detect_race_conditions=False)

    x_in = nc.dram_tensor("x_in", [T, H], F32, kind="ExternalInput")
    wg_in = nc.dram_tensor("wg_in", [I, 512], F32, kind="ExternalInput")
    wu_in = nc.dram_tensor("wu_in", [I, 512], F32, kind="ExternalInput")
    wd_in = nc.dram_tensor("wd_in", [512, I], F32, kind="ExternalInput")
    identb_in = nc.dram_tensor("identb_in", [P, P], BF16, kind="ExternalInput")
    identf_in = nc.dram_tensor("identf_in", [P, P], F32, kind="ExternalInput")
    ones_in = nc.dram_tensor("ones_in", [1, P], F32, kind="ExternalInput")
    out_d = nc.dram_tensor("out", [T, H], F32, kind="ExternalOutput")

    ar_in = nc.dram_tensor("ar_in", [1, 4], F32)
    ar_out = nc.dram_tensor("ar_out", [1, 4], F32)
    agg_in = [nc.dram_tensor(f"agg_in{g}", [P, sz * 512], BF16) for g, (s, sz) in enumerate(ggrp)]
    agg_out = [nc.dram_tensor(f"agg_out{g}", [P * NC, sz * 512], BF16) for g, (s, sz) in enumerate(ggrp)]
    agu_in = [nc.dram_tensor(f"agu_in{g}", [P, sz * 512], BF16) for g, (s, sz) in enumerate(ggrp)]
    agu_out = [nc.dram_tensor(f"agu_out{g}", [P * NC, sz * 512], BF16) for g, (s, sz) in enumerate(ggrp)]
    agd_in = [nc.dram_tensor(f"agd_in{g}", [P, sz * 512], BF16) for g, (s, sz) in enumerate(dgrp)]
    agd_out = [nc.dram_tensor(f"agd_out{g}", [P * NC, sz * 512], BF16) for g, (s, sz) in enumerate(dgrp)]
    x1_d = nc.dram_tensor("x1_d", [T, H], F32,
                          kind="ExternalOutput" if dbg else "Internal")
    e_d = nc.dram_tensor("e_d", [MT * P, T], F32,
                         kind="ExternalOutput" if dbg else "Internal")

    st = ExitStack()
    gd = Ctr(st, nc, "gd")
    cst = Ctr(st, nc, "cst")
    xs_ = [Ctr(st, nc, f"xs{i}") for i in range(2)]
    pa_ = [Ctr(st, nc, f"pa{i}") for i in range(2)]
    x1st = Ctr(st, nc, "x1st")
    stg_ = [Ctr(st, nc, f"stg{i}") for i in range(2)]
    gst = Ctr(st, nc, "gst")
    pb_ = [Ctr(st, nc, f"pbs{i}") for i in range(2)]
    pd_s = Ctr(st, nc, "pds")
    wb_ = [Ctr(st, nc, f"wbs{i}") for i in range(2)]
    est_ = [Ctr(st, nc, f"est{i}") for i in range(2)]
    et_ = [Ctr(st, nc, f"ets{i}") for i in range(2)]
    xd_ = [Ctr(st, nc, f"xds{i}") for i in range(2)]
    wd_ = [Ctr(st, nc, f"wds{i}") for i in range(2)]
    os_ = [Ctr(st, nc, f"oss{i}") for i in range(2)]
    pr_s = Ctr(st, nc, "prs")
    du_ = Ctr(st, nc, "dus")
    vv = Ctr(st, nc, "vv")
    ss = Ctr(st, nc, "ss")
    tt = Ctr(st, nc, "tt")
    se = Ctr(st, nc, "se")
    ve = Ctr(st, nc, "ve")
    t4 = Ctr(st, nc, "t4")
    vq = Ctr(st, nc, "vq")
    vd = Ctr(st, nc, "vd")
    vpc = Ctr(st, nc, "vpc")
    spc = Ctr(st, nc, "spc")
    csem = {}

    def new_coll_sem(name):
        csem[name] = Ctr(st, nc, f"cl_{name}")
        return csem[name]

    identb = st.enter_context(nc.sbuf_tensor("identb", [P, P], BF16))
    identf = st.enter_context(nc.sbuf_tensor("identf", [P, P], F32))
    onesf = st.enter_context(nc.sbuf_tensor("onesf", [1, P], F32))
    repscal = st.enter_context(nc.sbuf_tensor("repscal", [P, 8], F32))
    maxbuf = st.enter_context(nc.sbuf_tensor("maxbuf", [P, T], F32))
    minbuf = st.enter_context(nc.sbuf_tensor("minbuf", [P, T], F32))
    maxbuf2 = st.enter_context(nc.sbuf_tensor("maxbuf2", [P, T], F32))
    minbuf2 = st.enter_context(nc.sbuf_tensor("minbuf2", [P, T], F32))
    c3b = st.enter_context(nc.sbuf_tensor("c3b", [P, TT], F32))
    gam = st.enter_context(nc.sbuf_tensor("gam", [P, TT], F32))
    mec = st.enter_context(nc.sbuf_tensor("mec", [P, TT], F32))
    phic = st.enter_context(nc.sbuf_tensor("phic", [P, TT], F32))
    psic = st.enter_context(nc.sbuf_tensor("psic", [P, TT], F32))
    tmpa = st.enter_context(nc.sbuf_tensor("tmpa", [P, TT], F32))
    tmpb = st.enter_context(nc.sbuf_tensor("tmpb", [P, TT], F32))
    sm = [st.enter_context(nc.sbuf_tensor(f"sm{i}", [P, 1], F32)) for i in range(10)]
    c1536 = st.enter_context(nc.sbuf_tensor("c1536", [P, 1], F32))
    rowb = st.enter_context(nc.sbuf_tensor("rowb", [1, 16], F32))
    meansb = st.enter_context(nc.sbuf_tensor("meansb", [1, 4], F32))
    colsum = st.enter_context(nc.sbuf_tensor("colsum", [P, 80], F32))
    totb = st.enter_context(nc.sbuf_tensor("totb", [P, 4], F32))
    sums4 = st.enter_context(nc.sbuf_tensor("sums4", [4, 1], F32))
    arrow = st.enter_context(nc.sbuf_tensor("arrow", [1, 4], F32))

    stY = ExitStack()
    Y = stY.enter_context(nc.sbuf_tensor("Y", [P, KT * T], BF16))

    stA = ExitStack()
    xa = [stA.enter_context(nc.sbuf_tensor(f"xa{i}", [P, H], F32)) for i in range(2)]
    xcb = stA.enter_context(nc.sbuf_tensor("xcb", [P, H], F32))
    x1b = stA.enter_context(nc.sbuf_tensor("x1b", [P, H], F32))
    f16t = stA.enter_context(nc.sbuf_tensor("f16t", [P, H], F16))
    yqb = stA.enter_context(nc.sbuf_tensor("yqb", [P, H], BF16))
    RAWW = 2048
    raw = [stA.enter_context(nc.sbuf_tensor(f"raw{i}", [P, RAWW], F32)) for i in range(2)]
    pb = stA.enter_context(nc.sbuf_tensor("pb", [P, 1024], F32))
    nb = stA.enter_context(nc.sbuf_tensor("nb", [P, 1024], F32))
    ternb = stA.enter_context(nc.sbuf_tensor("ternb", [P, 1024], BF16))
    rawD = [stA.enter_context(nc.sbuf_tensor(f"rawD{i}", [P, 512], F32)) for i in range(4)]
    ternD = [stA.enter_context(nc.sbuf_tensor(f"ternD{i}", [P, 512], BF16)) for i in range(4)]
    stgW = [stA.enter_context(nc.sbuf_tensor(f"stgW{i}", [P, 512], BF16)) for i in range(2)]
    pstY = [stA.enter_context(nc.psum_tensor(f"pstY{i}", [P, 1024], BF16)) for i in range(2)]
    pstW = [stA.enter_context(nc.psum_tensor(f"pstW{i}", [P, 1024], BF16)) for i in range(2)]
    ptot = stA.enter_context(nc.psum_tensor("ptot", [4, P], F32))
    prep8 = stA.enter_context(nc.psum_tensor("prep8", [P, 8], F32))

    blk = st.enter_context(nc.Block())
    Tn, G, SY = nc.tensor, nc.gpsimd, nc.sync
    V = SerEng(nc.vector, vpc)
    S = SerEng(nc.scalar, spc)

    SY.dma_start(out=identb[:, :], in_=identb_in[:, :]).then_inc(cst.h, 16); cst.n += 16
    SY.dma_start(out=identf[:, :], in_=identf_in[:, :]).then_inc(cst.h, 16); cst.n += 16
    SY.dma_start(out=onesf[:, :], in_=ones_in[:, :]).then_inc(cst.h, 16); cst.n += 16
    V.memset(colsum[:, :], 0.0)
    V.memset(totb[:, :], 0.0)
    V.memset(maxbuf[:, :], 0.0)
    V.memset(minbuf[:, :], 0.0)
    V.memset(c1536[:, :], 1536.0)

    xload = {}
    def load_x(tau):
        c = xs_[tau % 2]
        S.dma_start(out=xa[tau % 2][:, :],
                    in_=bass.AP(x_in, tau * P * H, [[H, P], [1, H]]))\
         .then_inc(c.h, 16); c.n += 16
        xload[tau] = (c, c.n)
    load_x(0)
    if TT > 1:
        load_x(1)

    p0a_chunks = []
    for m, w_t in ((0, wg_in), (1, wu_in)):
        CR = 512
        for ci in range((I + CR - 1) // CR):
            r0 = ci * CR
            a = min(CR, I - r0) // P
            p0a_chunks.append(("gu", w_t, r0, a * 512))
    for hg in range(4):
        for sp in range((I + RAWW - 1) // RAWW):
            c0 = sp * RAWW
            p0a_chunks.append(("d", wd_in, hg * P * I + c0, min(RAWW, I - c0)))
    if phase_limit == 'pa':
        p0a_chunks = []
    assert len(p0a_chunks) <= 80
    p0a_red = {}
    p0a_next = [0]

    def emit_p0a(nmax):
        for _ in range(nmax):
            ci = p0a_next[0]
            if ci >= len(p0a_chunks):
                return
            p0a_next[0] += 1
            kind, w_t, off, width = p0a_chunks[ci]
            if ci >= 2:
                vv.wait(SY, p0a_red[ci - 2])
            if kind == "gu":
                a = width // 512
                dst = bass.AP(raw[ci % 2], 0, [[RAWW, P], [512, a], [1, 512]])
                src = bass.AP(w_t, off, [[512, P], [P * 512, a], [1, 512]])
            else:
                dst = bass.AP(raw[ci % 2], 0, [[RAWW, P], [1, width]])
                src = bass.AP(w_t, off, [[I, P], [1, width]])
            c = pa_[ci % 2]
            SY.dma_start(out=dst, in_=src).then_inc(c.h, 16); c.n += 16
            V.wait_ge(c.h, c.n)
            p0a_red[ci] = V.tensor_reduce(
                colsum[:, ci:ci + 1],
                bass.AP(raw[ci % 2], 0, [[RAWW, P], [1, width]]),
                AX.X, ALU.add, apply_absolute_value=True, _ctr=vv)

    inv_h = 1.0 / float(H)
    yq_copy = {}
    x1_dma = {}

    for tau in range(TT):
        m_mean, m_neg, m_var, m_aeps, m_sig, m_t, m_u, m_s1, m_rstd, m_x = sm
        xt = xa[tau % 2]

        def ln_stats(src):
            xc, xv = xload[tau]
            V.wait_ge(xc.h, xv)
            V.tensor_reduce(m_mean[:, :], src[:, :], AX.X, ALU.add)
            V.tensor_scalar(m_mean[:, :], m_mean[:, :], inv_h, None, ALU.mult)
            v1 = V.tensor_scalar(m_neg[:, :], m_mean[:, :], -1.0, None, ALU.mult, _ctr=vv)
            vv.wait(S, v1)
            S.activation(xcb[:, :], src[:, :], ACT.Identity, bias=m_neg[:, :], scale=1.0)
            s1 = S.activation(f16t[:, :], xcb[:, :], ACT.Square, accum_out=m_var[:, :], _ctr=ss)
            ss.wait(V, s1)
            V.tensor_scalar(m_var[:, :], m_var[:, :], inv_h, None, ALU.mult)
            v2 = V.tensor_scalar(m_aeps[:, :], m_var[:, :], 1e-5, None, ALU.add, _ctr=vv)
            vv.wait(S, v2)
            s2 = S.activation(m_sig[:, :], m_aeps[:, :], ACT.Sqrt, _ctr=ss)
            ss.wait(V, s2)
            V.reciprocal(m_t[:, :], m_sig[:, :])
            V.tensor_tensor(m_t[:, :], m_aeps[:, :], m_t[:, :], ALU.mult)
            V.tensor_tensor(m_u[:, :], m_sig[:, :], m_t[:, :], ALU.add)
            V.tensor_scalar(m_s1[:, :], m_u[:, :], 0.5, None, ALU.mult)
            V.reciprocal(m_rstd[:, :], m_s1[:, :])

        ln_stats(xt)
        V.tensor_scalar(m_x[:, :], m_rstd[:, :], 2.0, None, ALU.mult)
        v4 = V.scalar_tensor_tensor(x1b[:, :], xcb[:, :], m_x[:, :], xt[:, :],
                                    ALU.mult, ALU.add, _ctr=vv)
        vv.wait(S, v4)
        S.dma_start(out=bass.AP(x1_d, tau * P * H, [[H, P], [1, H]]),
                    in_=x1b[:, :]).then_inc(x1st.h, 16); x1st.n += 16
        x1_dma[tau] = x1st.n
        if tau + 2 < TT:
            load_x(tau + 2)

        ln_stats(x1b)
        V.tensor_reduce(m_t[:, :], xcb[:, :], AX.X, ALU.max, apply_absolute_value=True)
        V.tensor_scalar(m_u[:, :], m_t[:, :], m_rstd[:, :], 1e-5, ALU.mult, ALU.max)
        V.tensor_tensor(m_t[:, :], m_u[:, :], m_u[:, :], ALU.mult)
        V.tensor_tensor(c3b[:, tau:tau + 1], m_t[:, :], m_u[:, :], ALU.mult)
        V.reciprocal(m_t[:, :], m_u[:, :])
        V.tensor_scalar(m_t[:, :], m_t[:, :], 127.0, None, ALU.mult)
        v5 = V.tensor_tensor(m_x[:, :], m_t[:, :], m_rstd[:, :], ALU.mult, _ctr=vv)
        vv.wait(S, v5)
        x1st.wait(S, x1_dma[tau])
        S.activation(x1b[:, :], xcb[:, :], ACT.Identity, bias=c1536[:, :], scale=m_x[:, :])
        s3 = S.copy(f16t[:, :], x1b[:, :], _ctr=ss)
        ss.wait(V, s3)
        v6 = V.tensor_scalar(yqb[:, :], f16t[:, :], -1536.0, None, ALU.add, _ctr=vv)

        for b in range(KT // 4):
            if tau == 0 and b == 0:
                cst.wait(Tn, cst.n)
            vv.wait(Tn, v6)
            if (tau * (KT // 4) + b) >= 2:
                key = (tau * (KT // 4) + b) - 2
                vv.wait(Tn, yq_copy[key])
            for j in range(4):
                ins = Tn.matmul(pstY[b % 2][:, j * P:(j + 1) * P],
                                yqb[:, (b * 4 + j) * P:(b * 4 + j + 1) * P],
                                identb[:, :], is_transpose=True,
                                start=(j == 0), stop=(j == 3))
            tb_ = tt.inc(ins)
            tt.wait(V, tb_)
            yq_copy[tau * (KT // 4) + b] = V.tensor_copy(
                bass.AP(Y, (b * 4) * T + tau * P, [[KT * T, P], [T, 4], [1, P]]),
                pstY[b % 2][:, 0:512], _ctr=vv)

        emit_p0a(10)

    emit_p0a(len(p0a_chunks))
    v_Yfin = vv.n

    if phase_limit == 'pa':
        x1st.wait(SY, x1st.n)
        for tau in range(TT):
            SY.dma_start(out=bass.AP(out_d, tau * P * H, [[H, P], [1, H]]),
                         in_=bass.AP(x1_d, tau * P * H, [[H, P], [1, H]]))\
              .then_inc(du_.h, 16); du_.n += 16
        SY.wait_ge(du_.h, du_.n)
        S.wait_ge(du_.h, du_.n)
        stA.close()
        stY.close()
        st.close()
        return nc

    nch_g = (I + 511) // 512
    nd0 = 2 * nch_g
    V.tensor_reduce(totb[:, 0:1], colsum[:, 0:nch_g], AX.X, ALU.add)
    V.tensor_reduce(totb[:, 1:2], colsum[:, nch_g:2 * nch_g], AX.X, ALU.add)
    v_tot = V.tensor_reduce(totb[:, 2:3], colsum[:, nd0:p0a_next[0]], AX.X, ALU.add, _ctr=vv)
    vv.wait(Tn, v_tot)
    t_tot = tt.inc(Tn.transpose(ptot[:, :], totb[:, :], identf[:, :]))
    tt.wait(V, t_tot)
    v_sums = V.tensor_reduce(sums4[:, :], ptot[:, :], AX.X, ALU.add, _ctr=vv)
    vv.wait(G, v_sums)
    G.dma_start(out=bass.AP(ar_in, 0, [[4, 1], [1, 4]]),
                in_=bass.AP(sums4, 0, [[1, 4], [1, 1]]),
                allow_slow_non_contiguous=True).then_inc(gd.h, 16); gd.n += 16
    G.wait_ge(gd.h, gd.n)
    arsem = new_coll_sem("ar")
    arsem.inc(G.collective_compute("AllReduce", ALU.add,
                                   replica_groups=[list(range(NC))],
                                   ins=[ar_in.ap().opt()], outs=[ar_out.ap().opt()]))
    G.wait_ge(arsem.h, arsem.n)
    G.dma_start(out=arrow[:, :], in_=ar_out[:, :]).then_inc(gd.h, 16); gd.n += 16
    gd_ar = gd.n

    gd.wait(V, gd_ar)
    V.memset(rowb[:, 0:1], NTOT)
    V.memset(rowb[:, 1:2], 127.0)
    V.reciprocal(rowb[:, 2:3], rowb[:, 0:1])
    V.reciprocal(rowb[:, 3:4], rowb[:, 1:2])
    for m in range(3):
        V.tensor_tensor(meansb[:, m:m + 1], arrow[:, m:m + 1], rowb[:, 2:3], ALU.mult)
        V.tensor_scalar(rowb[:, 8 + m:9 + m], meansb[:, m:m + 1], 1e-5, None, ALU.max)
        V.reciprocal(rowb[:, 4 + m:5 + m], rowb[:, 8 + m:9 + m])
    V.tensor_tensor(rowb[:, 8:9], meansb[:, 0:1], meansb[:, 0:1], ALU.mult)
    V.tensor_tensor(rowb[:, 8:9], rowb[:, 8:9], meansb[:, 1:2], ALU.mult)
    V.tensor_tensor(rowb[:, 9:10], rowb[:, 3:4], rowb[:, 3:4], ALU.mult)
    V.tensor_tensor(rowb[:, 9:10], rowb[:, 9:10], rowb[:, 3:4], ALU.mult)
    V.tensor_tensor(rowb[:, 7:8], rowb[:, 8:9], rowb[:, 9:10], ALU.mult)
    V.tensor_tensor(rowb[:, 8:9], meansb[:, 2:3], rowb[:, 3:4], ALU.mult)
    V.tensor_copy(rowb[:, 0:1], rowb[:, 4:5])
    V.tensor_copy(rowb[:, 1:2], rowb[:, 5:6])
    V.tensor_copy(rowb[:, 2:3], rowb[:, 6:7])
    V.tensor_copy(rowb[:, 3:4], rowb[:, 7:8])
    v_row = V.tensor_copy(rowb[:, 4:5], rowb[:, 8:9], _ctr=vv)
    vv.wait(Tn, v_row)
    t_rep = tt.inc(Tn.matmul(prep8[:, :], onesf[:, :], rowb[:, 0:8], start=True, stop=True))
    tt.wait(V, t_rep)
    v_rep = V.tensor_copy(repscal[:, :], prep8[:, :], _ctr=vv)

    wuse = [0]
    stg_dma = {}
    stg_copy = {}

    def emit_tile_store(src_sb, src_w, src_off, ag_t, dst_off, agwidth, v_src):
        w = wuse[0]; wuse[0] += 1
        vv.wait(Tn, v_src)
        if w >= 2:
            vv.wait(Tn, stg_copy[w - 2])
        for j in range(4):
            ins = Tn.matmul(pstW[w % 2][:, j * P:(j + 1) * P],
                            bass.AP(src_sb, src_off + j * P, [[src_w, P], [1, P]]),
                            identb[:, :], is_transpose=True,
                            start=(j == 0), stop=(j == 3))
        tw = tt.inc(ins)
        tt.wait(V, tw)
        if w >= 2:
            sc, sv = stg_dma[w - 2]
            V.wait_ge(sc.h, sv)
        stg_copy[w] = V.tensor_copy(stgW[w % 2][:, :], pstW[w % 2][:, 0:512], _ctr=vv)
        vv.wait(S, stg_copy[w])
        c = stg_[w % 2]
        S.dma_start(out=bass.AP(ag_t, dst_off, [[agwidth, P], [1, 512]]),
                    in_=stgW[w % 2][:, :]).then_inc(c.h, 16); c.n += 16
        stg_dma[w] = (c, c.n)

    def publish_group_stores():
        S.wait_ge(stg_[0].h, stg_[0].n)
        S.wait_ge(stg_[1].h, stg_[1].n)
        S.activation(sm[1][:, :], sm[0][:, :], ACT.Identity, _ctr=gst)
        return gst.n

    rawuse = [0]
    tern_done = {}

    p0a_last_red = max(p0a_red.values()) if p0a_red else 0

    def p0b_gu(w_t, scal_col, ag_t, s0, sz):
        agw = sz * 512
        for mt0 in range(s0, s0 + sz, 2):
            cnt = min(2, s0 + sz - mt0)
            ru = rawuse[0]; rawuse[0] += 1
            if ru >= 2:
                SY.wait_ge(vv.h, tern_done[ru - 2])
            else:
                vv.wait(SY, p0a_last_red)
            c = pb_[ru % 2]
            SY.dma_start(out=bass.AP(raw[ru % 2], 0, [[RAWW, P], [512, cnt], [1, 512]]),
                         in_=bass.AP(w_t, mt0 * P * 512, [[512, P], [P * 512, cnt], [1, 512]]))\
              .then_inc(c.h, 16); c.n += 16
            V.wait_ge(c.h, c.n)
            vv.wait(V, v_rep)
            w = cnt * 512
            V.tensor_scalar(pb[:, 0:w], bass.AP(raw[ru % 2], 0, [[RAWW, P], [1, w]]),
                            repscal[:, scal_col:scal_col + 1], 0.5, ALU.mult, ALU.is_gt)
            V.tensor_scalar(nb[:, 0:w], bass.AP(raw[ru % 2], 0, [[RAWW, P], [1, w]]),
                            repscal[:, scal_col:scal_col + 1], -0.5, ALU.mult, ALU.is_lt)
            tern_done[ru] = V.tensor_tensor(ternb[:, 0:w], pb[:, 0:w], nb[:, 0:w],
                                            ALU.subtract, _ctr=vv)
            for k in range(cnt):
                emit_tile_store(ternb, 1024, k * 512, ag_t, (mt0 - s0 + k) * 512, agw, tern_done[ru])

    cg_g, cg_u = {}, {}
    for g, (s0, sz) in enumerate(ggrp):
        p0b_gu(wg_in, 0, agg_in[g], s0, sz)
        p0b_gu(wu_in, 1, agu_in[g], s0, sz)
        gp = publish_group_stores()
        gst.wait(G, gp)
        cg_g[g] = new_coll_sem(f"g{g}")
        cg_g[g].inc(G.collective_compute("AllGather", ALU.bypass,
                                         replica_groups=[list(range(NC))],
                                         ins=[agg_in[g].ap().opt()], outs=[agg_out[g].ap().opt()]))
        cg_u[g] = new_coll_sem(f"u{g}")
        cg_u[g].inc(G.collective_compute("AllGather", ALU.bypass,
                                         replica_groups=[list(range(NC))],
                                         ins=[agu_in[g].ap().opt()], outs=[agu_out[g].ap().opt()]))

    ternD_prev = [0]
    cg_d = {}
    for gdi, (s0, sz) in enumerate(dgrp):
        agw = sz * 512
        for kt0 in range(s0, s0 + sz, 4):
            cnt = min(4, s0 + sz - kt0)
            if ternD_prev[0]:
                vv.wait(SY, ternD_prev[0])
            for hg in range(4):
                SY.dma_start(out=bass.AP(rawD[hg], 0, [[512, P], [1, cnt * P]]),
                             in_=bass.AP(wd_in, hg * P * I + kt0 * P, [[I, P], [1, cnt * P]]))\
                  .then_inc(pd_s.h, 16); pd_s.n += 16
            pd_s.wait(V, pd_s.n)
            vv.wait(V, v_rep)
            for hg in range(4):
                w = cnt * P
                V.tensor_scalar(pb[:, 0:w], rawD[hg][:, 0:w],
                                repscal[:, 2:3], 0.5, ALU.mult, ALU.is_gt)
                V.tensor_scalar(nb[:, 0:w], rawD[hg][:, 0:w],
                                repscal[:, 2:3], -0.5, ALU.mult, ALU.is_lt)
                last = V.tensor_tensor(ternD[hg][:, 0:w], pb[:, 0:w], nb[:, 0:w],
                                       ALU.subtract, _ctr=(vv if hg == 3 else None))
            ternD_prev[0] = last
            for k in range(cnt):
                w = wuse[0]; wuse[0] += 1
                vv.wait(Tn, ternD_prev[0])
                if w >= 2:
                    vv.wait(Tn, stg_copy[w - 2])
                for ml in range(4):
                    ins = Tn.matmul(pstW[w % 2][:, ml * P:(ml + 1) * P],
                                    ternD[ml][:, k * P:(k + 1) * P], identb[:, :],
                                    is_transpose=True, start=(ml == 0), stop=(ml == 3))
                tw = tt.inc(ins)
                tt.wait(V, tw)
                if w >= 2:
                    sc, sv = stg_dma[w - 2]
                    V.wait_ge(sc.h, sv)
                stg_copy[w] = V.tensor_copy(stgW[w % 2][:, :], pstW[w % 2][:, 0:512], _ctr=vv)
                vv.wait(S, stg_copy[w])
                c = stg_[w % 2]
                S.dma_start(out=bass.AP(agd_in[gdi], (kt0 - s0 + k) * 512, [[agw, P], [1, 512]]),
                            in_=stgW[w % 2][:, :]).then_inc(c.h, 16); c.n += 16
                stg_dma[w] = (c, c.n)
        gp = publish_group_stores()
        gst.wait(G, gp)
        cg_d[gdi] = new_coll_sem(f"d{gdi}")
        cg_d[gdi].inc(G.collective_compute("AllGather", ALU.bypass,
                                           replica_groups=[list(range(NC))],
                                           ins=[agd_in[gdi].ap().opt()], outs=[agd_out[gdi].ap().opt()]))

    v_p0b = vv.n
    t_p0b = tt.n
    g_p0b = gst.n

    def _dummy_out():
        x1st.wait(SY, x1st.n)
        for tau in range(TT):
            SY.dma_start(out=bass.AP(out_d, tau * P * H, [[H, P], [1, H]]),
                         in_=bass.AP(x1_d, tau * P * H, [[H, P], [1, H]]))\
              .then_inc(du_.h, 16); du_.n += 16
        SY.wait_ge(du_.h, du_.n)
        S.wait_ge(du_.h, du_.n)
        for cs in csem.values():
            G.wait_ge(cs.h, cs.n)
        G.wait_ge(gd.h, gd.n)

    if phase_limit == 'p0':
        _dummy_out()
        if dbg:
            for cs in csem.values():
                SY.wait_ge(cs.h, cs.n)
            for fam, outs, grp in (("dmpg", agg_out, ggrp), ("dmpu", agu_out, ggrp),
                                   ("dmpd", agd_out, dgrp)):
                for g_, (s_, sz_) in enumerate(grp):
                    td = nc.dram_tensor(f"{fam}{g_}", [P * NC, sz_ * 512], BF16,
                                        kind="ExternalOutput")
                    SY.dma_start(out=td[:, :], in_=outs[g_][:, :]).then_inc(du_.h, 16)
                    du_.n += 16
            for fam, ins_, grp in (("dmpig", agg_in, ggrp), ("dmpiu", agu_in, ggrp),
                                   ("dmpid", agd_in, dgrp)):
                for g_, (s_, sz_) in enumerate(grp):
                    td = nc.dram_tensor(f"{fam}{g_}", [P, sz_ * 512], BF16,
                                        kind="ExternalOutput")
                    SY.dma_start(out=td[:, :], in_=ins_[g_][:, :]).then_inc(du_.h, 16)
                    du_.n += 16
            SY.wait_ge(du_.h, du_.n)
        stA.close()
        stY.close()
        st.close()
        return nc

    stA.close()
    stM = ExitStack()
    wbuf = [stM.enter_context(nc.sbuf_tensor(f"wbuf{i}", [P, 2 * KT * P], BF16)) for i in range(2)]
    rgb = [stM.enter_context(nc.sbuf_tensor(f"rgb{i}", [P, T], F32)) for i in range(2)]
    rg2 = [stM.enter_context(nc.sbuf_tensor(f"rg2{i}", [P, T], F32)) for i in range(2)]
    ebuf = [stM.enter_context(nc.sbuf_tensor(f"ebuf{i}", [P, T], F32)) for i in range(2)]
    gps = [stM.enter_context(nc.psum_tensor(f"gps{i}", [P, T], F32)) for i in range(2)]
    ups = [stM.enter_context(nc.psum_tensor(f"ups{i}", [P, T], F32)) for i in range(2)]

    mt_group = {}
    for g, (s0, sz) in enumerate(ggrp):
        for mt in range(s0, s0 + sz):
            mt_group[mt] = g
    wb_dma = {}
    T_MAIN0 = tt.n
    e_dma = {}
    for mt in range(MT):
        g = mt_group[mt]
        s0, sz = ggrp[g]
        SY.wait_ge(cg_g[g].h, cg_g[g].n)
        SY.wait_ge(cg_u[g].h, cg_u[g].n)
        if mt == 0:
            vv.wait(SY, v_p0b)
            tt.wait(SY, t_p0b)
            gst.wait(SY, g_p0b)
        if mt >= 2:
            tt.wait(SY, T_MAIN0 + mt - 1)
        cwb = wb_[mt % 2]
        for ag_t, coff in ((agg_out[g], 0), (agu_out[g], KT * P)):
            SY.dma_start(
                out=bass.AP(wbuf[mt % 2], coff, [[2 * KT * P, P], [512, NC], [1, 512]]),
                in_=bass.AP(ag_t, (mt - s0) * 512, [[sz * 512, P], [P * sz * 512, NC], [1, 512]]))\
              .then_inc(cwb.h, 16); cwb.n += 16
        wb_dma[mt] = (cwb, cwb.n)
        Tn.wait_ge(cwb.h, cwb.n)
        if mt == 0:
            vv.wait(Tn, v_Yfin)
        if mt >= 2:
            se.wait(Tn, mt - 1)
            ve.wait(Tn, mt - 1)
        for kt in range(KT):
            st0, sp0 = (kt == 0), (kt == KT - 1)
            for th in range(NH):
                ins = Tn.matmul(gps[mt % 2][:, th * 512:(th + 1) * 512],
                                wbuf[mt % 2][:, kt * P:(kt + 1) * P],
                                Y[:, kt * T + th * 512:kt * T + (th + 1) * 512],
                                start=st0, stop=sp0)
            for th in range(NH):
                ins = Tn.matmul(ups[mt % 2][:, th * 512:(th + 1) * 512],
                                wbuf[mt % 2][:, KT * P + kt * P:KT * P + (kt + 1) * P],
                                Y[:, kt * T + th * 512:kt * T + (th + 1) * 512],
                                start=st0, stop=sp0)
        tmm = tt.inc(ins)
        tt.wait(S, tmm)
        S.activation(rgb[mt % 2][:, :], gps[mt % 2][:, :], ACT.Relu)
        S.activation(rg2[mt % 2][:, :], rgb[mt % 2][:, :], ACT.Square, _ctr=se)
        se.wait(V, mt + 1)
        if mt >= 2:
            ec, ev = e_dma[mt - 2]
            V.wait_ge(ec.h, ev)
        V.tensor_tensor(ebuf[mt % 2][:, :], rg2[mt % 2][:, :], ups[mt % 2][:, :], ALU.mult)
        V.tensor_tensor(maxbuf2[:, :], ebuf[mt % 2][:, :], maxbuf[:, :], ALU.max)
        V.tensor_tensor(minbuf2[:, :], ebuf[mt % 2][:, :], minbuf[:, :], ALU.min, _ctr=ve)
        maxbuf, maxbuf2 = maxbuf2, maxbuf
        minbuf, minbuf2 = minbuf2, minbuf
        ve.wait(S, mt + 1)
        ce = est_[mt % 2]
        S.dma_start(out=bass.AP(e_d, mt * P * T, [[T, P], [1, T]]),
                    in_=ebuf[mt % 2][:, :]).then_inc(ce.h, 16); ce.n += 16
        e_dma[mt] = (ce, ce.n)

    if phase_limit == 'main':
        _dummy_out()
        stM.close()
        stY.close()
        st.close()
        return nc
    if dbg:
        Y_out = nc.dram_tensor("dbg_Y", [P, KT * T], BF16, kind="ExternalOutput")
        SY.wait_ge(vv.h, v_Yfin)
        SY.dma_start(out=Y_out[:, :], in_=Y[:, :]).then_inc(du_.h, 16); du_.n += 16
    stM.close()
    stY.close()

    psirep = st.enter_context(nc.sbuf_tensor("psirep", [P, T], F32))
    stP = ExitStack()
    pmax = stP.enter_context(nc.psum_tensor("pmax", [P, 512], F32))
    psit = stP.enter_context(nc.psum_tensor("psit", [TT, P], F32))
    prep2 = stP.enter_context(nc.psum_tensor("prep2", [P, 512], F32))
    psiT = stP.enter_context(nc.sbuf_tensor("psiT", [TT, P], F32))
    psirow = stP.enter_context(nc.sbuf_tensor("psirow", [1, T], F32))

    vv.wait(V, v_rep)
    v_gam = V.tensor_scalar(gam[:, :], c3b[:, :], repscal[:, 3:4], None, ALU.mult, _ctr=vv)
    ve.wait(V, MT)
    V.tensor_scalar(minbuf2[:, :], minbuf[:, :], -1.0, None, ALU.mult)
    v_fold = V.tensor_tensor(maxbuf2[:, :], maxbuf[:, :], minbuf2[:, :], ALU.max, _ctr=vv)
    maxbuf = maxbuf2
    last_red = None
    for tau in range(TT):
        vv.wait(Tn, v_fold)
        if last_red is not None:
            vv.wait(Tn, last_red)
        ttr = tt.inc(Tn.transpose(pmax[:, 0:P], maxbuf[:, tau * P:(tau + 1) * P], identf[:, :]))
        tt.wait(V, ttr)
        last_red = V.tensor_reduce(mec[:, tau:tau + 1], pmax[:, 0:P], AX.X, ALU.max, _ctr=vv)
    V.tensor_tensor(tmpa[:, :], gam[:, :], mec[:, :], ALU.mult)
    V.tensor_scalar(tmpa[:, :], tmpa[:, :], 1e-5, None, ALU.max)
    V.tensor_scalar(phic[:, :], tmpa[:, :], repscal[:, 4:5], None, ALU.mult)
    V.reciprocal(tmpb[:, :], tmpa[:, :])
    V.tensor_scalar(tmpb[:, :], tmpb[:, :], 127.0, None, ALU.mult)
    v_psi = V.tensor_tensor(psic[:, :], tmpb[:, :], gam[:, :], ALU.mult, _ctr=vv)
    vv.wait(Tn, v_psi)
    t_pt = tt.inc(Tn.transpose(psit[:, :], psic[:, :], identf[:, :]))
    tt.wait(V, t_pt)
    v_pT = V.tensor_copy(psiT[:, :], psit[:, :], _ctr=vv)
    vv.wait(SY, v_pT)
    SY.dma_start(out=bass.AP(psirow, 0, [[T, 1], [P, TT], [1, P]]),
                 in_=psiT[:, :]).then_inc(pr_s.h, 16); pr_s.n += 16
    v_prep_last = 0
    for j in range(T // 512):
        pr_s.wait(Tn, pr_s.n)
        if v_prep_last:
            vv.wait(Tn, v_prep_last)
        t_r = tt.inc(Tn.matmul(prep2[:, :], onesf[:, :], psirow[:, j * 512:(j + 1) * 512],
                               start=True, stop=True))
        tt.wait(V, t_r)
        v_prep_last = V.tensor_copy(psirep[:, j * 512:(j + 1) * 512], prep2[:, :], _ctr=vv)
    v_psirep = v_prep_last
    stP.close()

    if dbg:
        SY.wait_ge(vv.h, v_psirep)
        for nm, sb, rr, cc_ in (("dbgE_maxbuf", maxbuf, P, T), ("dbgE_mec", mec, P, TT),
                                ("dbgE_gam", gam, P, TT), ("dbgE_phic", phic, P, TT),
                                ("dbgE_psic", psic, P, TT)):
            tdb = nc.dram_tensor(nm, [rr, cc_], F32, kind="ExternalOutput")
            SY.dma_start(out=tdb[:, :], in_=sb[:, :]).then_inc(du_.h, 16); du_.n += 16

    if phase_limit == 'psi':
        vv.wait(SY, v_psirep)
        _dummy_out()
        st.close()
        return nc

    stD = ExitStack()
    Hq = stD.enter_context(nc.sbuf_tensor("Hq", [P, MT * 512], BF16))
    etile = [stD.enter_context(nc.sbuf_tensor(f"etile{i}", [P, 512], F32)) for i in range(2)]
    vb2 = stD.enter_context(nc.sbuf_tensor("vb2", [P, 512], F32))
    fb2 = stD.enter_context(nc.sbuf_tensor("fb2", [P, 512], F16))
    wdbW = max(sz for _, sz in dgrp) * 512
    wdb = [stD.enter_context(nc.sbuf_tensor(f"wdb{i}", [P, wdbW], BF16)) for i in range(2)]
    x1s = [[stD.enter_context(nc.sbuf_tensor(f"x1s{i}_{t}", [P, 512], F32)) for t in range(TG)]
           for i in range(2)]
    outb = [stD.enter_context(nc.sbuf_tensor(f"outb{i}", [P, TG * 512], F32)) for i in range(2)]
    dps = [stD.enter_context(nc.psum_tensor(f"dps{i}", [P, TG * 512], F32)) for i in range(2)]

    est_[0].wait(SY, est_[0].n)
    est_[1].wait(SY, est_[1].n)
    x1st.wait(SY, x1st.n)
    n_round = 0
    out_dma = {}
    t3_blocks = []
    for g2 in range(NTG):
        for kt in range(MT):
            if kt >= 2:
                vq.wait(SY, g2 * MT + kt - 1)
            cet = et_[kt % 2]
            SY.dma_start(out=etile[kt % 2][:, :],
                         in_=bass.AP(e_d, kt * P * T + g2 * TG * P, [[T, P], [1, 512]]))\
              .then_inc(cet.h, 16); cet.n += 16
            if g2 == 1:
                for hi, cnt_ in t3_blocks:
                    if kt <= hi:
                        t4.wait(V, cnt_)
                        break
            V.wait_ge(cet.h, cet.n)
            if g2 == 0 and kt == 0:
                vv.wait(V, v_psirep)
            V.tensor_tensor(vb2[:, :], etile[kt % 2][:, :],
                            psirep[:, g2 * TG * P:(g2 * TG + TG) * P], ALU.mult)
            V.tensor_scalar(vb2[:, :], vb2[:, :], 1536.0, None, ALU.add)
            V.tensor_copy(fb2[:, :], vb2[:, :])
            V.tensor_scalar(Hq[:, kt * 512:(kt + 1) * 512], fb2[:, :],
                            -1536.0, None, ALU.add, _ctr=vq)
        for hq in range(HQ):
            par = n_round % 2
            if n_round >= 2:
                vd.wait(SY, n_round - 1)
            cx1 = xd_[par]
            for tl in range(TG):
                SY.dma_start(out=x1s[par][tl][:, :],
                             in_=bass.AP(x1_d, (g2 * TG + tl) * P * H + hq * 512,
                                         [[H, P], [1, 512]]))\
                  .then_inc(cx1.h, 16); cx1.n += 16
            x1_cnt = cx1.n
            if n_round >= 2:
                vd.wait(Tn, n_round - 1)
            for gdi, (s0, sz) in enumerate(dgrp):
                seg = n_round * NGD + gdi
                wpar = seg % 2
                SY.wait_ge(cg_d[gdi].h, cg_d[gdi].n)
                if seg >= 2:
                    t4.wait(SY, seg - 1)
                cwd = wd_[wpar]
                SY.dma_start(out=bass.AP(wdb[wpar], 0, [[wdbW, P], [1, sz * 512]]),
                             in_=bass.AP(agd_out[gdi], hq * P * sz * 512,
                                         [[sz * 512, P], [1, sz * 512]]))\
                  .then_inc(cwd.h, 16); cwd.n += 16
                Tn.wait_ge(cwd.h, cwd.n)
                for k in range(sz):
                    kt = s0 + k
                    vq.wait(Tn, g2 * MT + kt + 1)
                    for tl in range(TG):
                        ins = Tn.matmul(dps[par][:, tl * 512:(tl + 1) * 512],
                                        Hq[:, kt * 512 + tl * P:kt * 512 + (tl + 1) * P],
                                        wdb[wpar][:, k * 512:(k + 1) * 512],
                                        start=(kt == 0), stop=(kt == MT - 1))
                t4.inc(ins)
                if g2 == 0 and hq == HQ - 1:
                    t3_blocks.append((s0 + sz - 1, t4.n))
            t4.wait(V, (n_round + 1) * NGD)
            V.wait_ge(cx1.h, x1_cnt)
            if n_round >= 2:
                oc, ov = out_dma[n_round - 2]
                V.wait_ge(oc.h, ov)
            for tl in range(TG):
                v_dr = V.scalar_tensor_tensor(
                    outb[par][:, tl * 512:(tl + 1) * 512],
                    dps[par][:, tl * 512:(tl + 1) * 512],
                    phic[:, g2 * TG + tl:g2 * TG + tl + 1],
                    x1s[par][tl][:, :], ALU.mult, ALU.add,
                    _ctr=(vd if tl == TG - 1 else None))
            vd.wait(S, v_dr)
            cos = os_[par]
            for tl in range(TG):
                S.dma_start(out=bass.AP(out_d, (g2 * TG + tl) * P * H + hq * 512,
                                        [[H, P], [1, 512]]),
                            in_=outb[par][:, tl * 512:(tl + 1) * 512])\
                  .then_inc(cos.h, 16); cos.n += 16
            out_dma[n_round] = (cos, cos.n)
            n_round += 1

    stD.close()

    if dbg:
        SY.wait_ge(vd.h, n_round)
        for nm, sb, rr, cc_ in (("dbg_maxbuf", maxbuf, P, T), ("dbg_mec", mec, P, TT),
                                ("dbg_gam", gam, P, TT), ("dbg_phic", phic, P, TT),
                                ("dbg_psic", psic, P, TT), ("dbg_rep", repscal, P, 8),
                                ("dbg_psirep", psirep, P, T)):
            tdb = nc.dram_tensor(nm, [rr, cc_], F32, kind="ExternalOutput")
            SY.dma_start(out=tdb[:, :], in_=sb[:, :]).then_inc(du_.h, 16); du_.n += 16

    S.wait_ge(os_[0].h, os_[0].n)
    S.wait_ge(os_[1].h, os_[1].n)
    SY.wait_ge(os_[0].h, os_[0].n)
    SY.wait_ge(os_[1].h, os_[1].n)
    if du_.n:
        SY.wait_ge(du_.h, du_.n)
    for cs in csem.values():
        G.wait_ge(cs.h, cs.n)
    G.wait_ge(gd.h, gd.n)

    st.close()
    return nc


def kernel(L0_attn_x, L0_ffn_w_gate, L0_ffn_w_up, L0_ffn_w_down):
    x = np.asarray(L0_attn_x, dtype=np.float32)
    wg = np.asarray(L0_ffn_w_gate, dtype=np.float32)
    wu = np.asarray(L0_ffn_w_up, dtype=np.float32)
    wd = np.asarray(L0_ffn_w_down, dtype=np.float32)
    B, Seq, Hm = x.shape
    I = wg.shape[0]
    T = (B * Seq) // NC
    nc = build_nc(I=I, T=T, NG=8, NGD=8)
    identb = np.eye(P, dtype=ml_dtypes.bfloat16)
    identf = np.eye(P, dtype=np.float32)
    ones = np.ones((1, P), np.float32)
    x2 = x.reshape(B * Seq, Hm)
    hs = Hm // NC
    in_maps = []
    for r in range(NC):
        in_maps.append({
            "x_in": np.ascontiguousarray(x2[r * T:(r + 1) * T]),
            "wg_in": np.ascontiguousarray(wg[:, r * hs:(r + 1) * hs]),
            "wu_in": np.ascontiguousarray(wu[:, r * hs:(r + 1) * hs]),
            "wd_in": np.ascontiguousarray(wd[r * hs:(r + 1) * hs, :]),
            "identb_in": identb, "identf_in": identf, "ones_in": ones,
        })
    res = run_bass_kernel_spmd(nc, in_maps, core_ids=list(range(NC)))
    out = np.concatenate([np.asarray(res.results[r]["out"]) for r in range(NC)], axis=0)
    return out.reshape(B, Seq, Hm).astype(np.float32)


# version 09
# speedup vs baseline: 1.1670x; 1.1670x over previous
import numpy as np
import ml_dtypes
from contextlib import ExitStack

import concourse.bass as bass
import concourse.mybir as mybir
from concourse.bass_utils import run_bass_kernel_spmd

F32 = mybir.dt.float32
BF16 = mybir.dt.bfloat16
F16 = mybir.dt.float16
ALU = mybir.AluOpType
ACT = mybir.ActivationFunctionType
AX = mybir.AxisListType

H = 4096
NC = 8
P = 128


def _groups(n, g):
    base, rem = divmod(n, g)
    sizes = [base + (1 if i < rem else 0) for i in range(g)]
    starts = [int(x) for x in np.cumsum([0] + sizes[:-1])]
    return list(zip(starts, sizes))


class Ctr:
    def __init__(self, st, nc, name):
        self.h = st.enter_context(nc.semaphore(name))
        self.n = 0

    def inc(self, instr, k=1):
        instr.then_inc(self.h, k)
        self.n += k
        return self.n

    def wait(self, eng, v):
        assert 0 <= v <= self.n, f"{self.h}: wait on future count {v} > {self.n}"
        if v > 0:
            eng.wait_ge(self.h, v)


class SerEng:

    COMPUTE = frozenset({
        "memset", "memzero", "tensor_copy", "tensor_tensor", "tensor_add",
        "tensor_sub", "tensor_mul", "tensor_max", "tensor_scalar",
        "tensor_single_scalar", "tensor_scalar_add", "tensor_scalar_sub",
        "tensor_scalar_mul", "tensor_scalar_max", "tensor_scalar_min",
        "tensor_relu", "tensor_reduce", "reduce_max", "reduce_sum",
        "scalar_tensor_tensor", "reciprocal", "transpose", "bn_stats",
        "bn_aggr", "activation", "copy", "add", "mul", "square", "sqrt",
        "sign", "select",
    })

    def __init__(self, eng, priv):
        self._e = eng
        self._p = priv
        self._last = None

    def __getattr__(self, name):
        attr = getattr(self._e, name)
        if name not in SerEng.COMPUTE:
            return attr

        def shim(*a, _ctr=None, **kw):
            if self._last is not None:
                c, v = self._last
                c.wait(self._e, v)
            ins = attr(*a, **kw)
            c = _ctr if _ctr is not None else self._p
            v = c.inc(ins)
            self._last = (c, v)
            return v
        return shim


def build_nc(I=11008, T=1024, NG=8, NGD=8, phase_limit='full', dbg=False):
    KT = H // P
    MT = I // P
    TT = T // P
    TG = 4 if TT >= 8 else TT
    NTG = TT // TG
    HQ = H // 512
    NH = T // 512
    ggrp = _groups(MT, NG)
    dgrp = _groups(MT, NGD)
    NTOT = float(I) * float(H)

    nc = bass.Bass("TRN2", target_bir_lowering=False, debug=False,
                   detect_race_conditions=False)

    x_in = nc.dram_tensor("x_in", [T, H], F32, kind="ExternalInput")
    wg_in = nc.dram_tensor("wg_in", [I, 512], F32, kind="ExternalInput")
    wu_in = nc.dram_tensor("wu_in", [I, 512], F32, kind="ExternalInput")
    wd_in = nc.dram_tensor("wd_in", [512, I], F32, kind="ExternalInput")
    identb_in = nc.dram_tensor("identb_in", [P, P], BF16, kind="ExternalInput")
    identf_in = nc.dram_tensor("identf_in", [P, P], F32, kind="ExternalInput")
    ones_in = nc.dram_tensor("ones_in", [1, P], F32, kind="ExternalInput")
    out_d = nc.dram_tensor("out", [T, H], F32, kind="ExternalOutput")

    ar_in = nc.dram_tensor("ar_in", [1, 4], F32)
    ar_out = nc.dram_tensor("ar_out", [1, 4], F32)
    agg_in = [nc.dram_tensor(f"agg_in{g}", [P, sz * 512], BF16) for g, (s, sz) in enumerate(ggrp)]
    agg_out = [nc.dram_tensor(f"agg_out{g}", [P * NC, sz * 512], BF16, addr_space="Shared") for g, (s, sz) in enumerate(ggrp)]
    agu_in = [nc.dram_tensor(f"agu_in{g}", [P, sz * 512], BF16) for g, (s, sz) in enumerate(ggrp)]
    agu_out = [nc.dram_tensor(f"agu_out{g}", [P * NC, sz * 512], BF16, addr_space="Shared") for g, (s, sz) in enumerate(ggrp)]
    agd_in = [nc.dram_tensor(f"agd_in{g}", [P, sz * 512], BF16) for g, (s, sz) in enumerate(dgrp)]
    agd_out = [nc.dram_tensor(f"agd_out{g}", [P * NC, sz * 512], BF16, addr_space="Shared") for g, (s, sz) in enumerate(dgrp)]
    x1_d = nc.dram_tensor("x1_d", [T, H], F32,
                          kind="ExternalOutput" if dbg else "Internal")
    e_d = nc.dram_tensor("e_d", [MT * P, T], F32,
                         kind="ExternalOutput" if dbg else "Internal")

    st = ExitStack()
    gd = Ctr(st, nc, "gd")
    cst = Ctr(st, nc, "cst")
    xs_ = [Ctr(st, nc, f"xs{i}") for i in range(2)]
    pa_ = [Ctr(st, nc, f"pa{i}") for i in range(2)]
    x1st = Ctr(st, nc, "x1st")
    stg_ = [Ctr(st, nc, f"stg{i}") for i in range(2)]
    gst = Ctr(st, nc, "gst")
    pb_ = [Ctr(st, nc, f"pbs{i}") for i in range(2)]
    pd_s = Ctr(st, nc, "pds")
    wb_ = [Ctr(st, nc, f"wbs{i}") for i in range(2)]
    est_ = [Ctr(st, nc, f"est{i}") for i in range(2)]
    et_ = [Ctr(st, nc, f"ets{i}") for i in range(2)]
    xd_ = [Ctr(st, nc, f"xds{i}") for i in range(2)]
    wd_ = [Ctr(st, nc, f"wds{i}") for i in range(2)]
    os_ = [Ctr(st, nc, f"oss{i}") for i in range(2)]
    pr_s = Ctr(st, nc, "prs")
    du_ = Ctr(st, nc, "dus")
    vv = Ctr(st, nc, "vv")
    ss = Ctr(st, nc, "ss")
    tt = Ctr(st, nc, "tt")
    se = Ctr(st, nc, "se")
    ve = Ctr(st, nc, "ve")
    t4 = Ctr(st, nc, "t4")
    vq = Ctr(st, nc, "vq")
    vd = Ctr(st, nc, "vd")
    vpc = Ctr(st, nc, "vpc")
    spc = Ctr(st, nc, "spc")
    csem = {}

    def new_coll_sem(name):
        csem[name] = Ctr(st, nc, f"cl_{name}")
        return csem[name]

    identb = st.enter_context(nc.sbuf_tensor("identb", [P, P], BF16))
    identf = st.enter_context(nc.sbuf_tensor("identf", [P, P], F32))
    onesf = st.enter_context(nc.sbuf_tensor("onesf", [1, P], F32))
    repscal = st.enter_context(nc.sbuf_tensor("repscal", [P, 8], F32))
    maxbuf = st.enter_context(nc.sbuf_tensor("maxbuf", [P, T], F32))
    minbuf = st.enter_context(nc.sbuf_tensor("minbuf", [P, T], F32))
    maxbuf2 = st.enter_context(nc.sbuf_tensor("maxbuf2", [P, T], F32))
    minbuf2 = st.enter_context(nc.sbuf_tensor("minbuf2", [P, T], F32))
    c3b = st.enter_context(nc.sbuf_tensor("c3b", [P, TT], F32))
    gam = st.enter_context(nc.sbuf_tensor("gam", [P, TT], F32))
    mec = st.enter_context(nc.sbuf_tensor("mec", [P, TT], F32))
    phic = st.enter_context(nc.sbuf_tensor("phic", [P, TT], F32))
    psic = st.enter_context(nc.sbuf_tensor("psic", [P, TT], F32))
    tmpa = st.enter_context(nc.sbuf_tensor("tmpa", [P, TT], F32))
    tmpb = st.enter_context(nc.sbuf_tensor("tmpb", [P, TT], F32))
    sm = [st.enter_context(nc.sbuf_tensor(f"sm{i}", [P, 1], F32)) for i in range(10)]
    c1536 = st.enter_context(nc.sbuf_tensor("c1536", [P, 1], F32))
    rowb = st.enter_context(nc.sbuf_tensor("rowb", [1, 16], F32))
    meansb = st.enter_context(nc.sbuf_tensor("meansb", [1, 4], F32))
    colsum = st.enter_context(nc.sbuf_tensor("colsum", [P, 80], F32))
    totb = st.enter_context(nc.sbuf_tensor("totb", [P, 4], F32))
    sums4 = st.enter_context(nc.sbuf_tensor("sums4", [4, 1], F32))
    arrow = st.enter_context(nc.sbuf_tensor("arrow", [1, 4], F32))

    stY = ExitStack()
    Y = stY.enter_context(nc.sbuf_tensor("Y", [P, KT * T], BF16))

    stA = ExitStack()
    xa = [stA.enter_context(nc.sbuf_tensor(f"xa{i}", [P, H], F32)) for i in range(2)]
    xcb = stA.enter_context(nc.sbuf_tensor("xcb", [P, H], F32))
    x1b = stA.enter_context(nc.sbuf_tensor("x1b", [P, H], F32))
    f16t = stA.enter_context(nc.sbuf_tensor("f16t", [P, H], F16))
    yqb = stA.enter_context(nc.sbuf_tensor("yqb", [P, H], BF16))
    RAWW = 2048
    raw = [stA.enter_context(nc.sbuf_tensor(f"raw{i}", [P, RAWW], F32)) for i in range(2)]
    pb = stA.enter_context(nc.sbuf_tensor("pb", [P, 1024], F32))
    nb = stA.enter_context(nc.sbuf_tensor("nb", [P, 1024], F32))
    ternb = stA.enter_context(nc.sbuf_tensor("ternb", [P, 1024], BF16))
    rawD = [stA.enter_context(nc.sbuf_tensor(f"rawD{i}", [P, 512], F32)) for i in range(4)]
    ternD = [stA.enter_context(nc.sbuf_tensor(f"ternD{i}", [P, 512], BF16)) for i in range(4)]
    stgW = [stA.enter_context(nc.sbuf_tensor(f"stgW{i}", [P, 512], BF16)) for i in range(2)]
    pstY = [stA.enter_context(nc.psum_tensor(f"pstY{i}", [P, 1024], BF16)) for i in range(2)]
    pstW = [stA.enter_context(nc.psum_tensor(f"pstW{i}", [P, 1024], BF16)) for i in range(2)]
    ptot = stA.enter_context(nc.psum_tensor("ptot", [4, P], F32))
    prep8 = stA.enter_context(nc.psum_tensor("prep8", [P, 8], F32))

    blk = st.enter_context(nc.Block())
    Tn, G, SY = nc.tensor, nc.gpsimd, nc.sync
    V = SerEng(nc.vector, vpc)
    S = SerEng(nc.scalar, spc)

    SY.dma_start(out=identb[:, :], in_=identb_in[:, :]).then_inc(cst.h, 16); cst.n += 16
    SY.dma_start(out=identf[:, :], in_=identf_in[:, :]).then_inc(cst.h, 16); cst.n += 16
    SY.dma_start(out=onesf[:, :], in_=ones_in[:, :]).then_inc(cst.h, 16); cst.n += 16
    V.memset(colsum[:, :], 0.0)
    V.memset(totb[:, :], 0.0)
    V.memset(maxbuf[:, :], 0.0)
    V.memset(minbuf[:, :], 0.0)
    V.memset(c1536[:, :], 1536.0)

    xload = {}
    def load_x(tau):
        c = xs_[tau % 2]
        S.dma_start(out=xa[tau % 2][:, :],
                    in_=bass.AP(x_in, tau * P * H, [[H, P], [1, H]]))\
         .then_inc(c.h, 16); c.n += 16
        xload[tau] = (c, c.n)
    load_x(0)
    if TT > 1:
        load_x(1)

    p0a_chunks = []
    for m, w_t in ((0, wg_in), (1, wu_in)):
        CR = 512
        for ci in range((I + CR - 1) // CR):
            r0 = ci * CR
            a = min(CR, I - r0) // P
            p0a_chunks.append(("gu", w_t, r0, a * 512))
    for hg in range(4):
        for sp in range((I + RAWW - 1) // RAWW):
            c0 = sp * RAWW
            p0a_chunks.append(("d", wd_in, hg * P * I + c0, min(RAWW, I - c0)))
    if phase_limit == 'pa':
        p0a_chunks = []
    assert len(p0a_chunks) <= 80
    p0a_red = {}
    p0a_next = [0]

    def emit_p0a(nmax):
        for _ in range(nmax):
            ci = p0a_next[0]
            if ci >= len(p0a_chunks):
                return
            p0a_next[0] += 1
            kind, w_t, off, width = p0a_chunks[ci]
            if ci >= 2:
                vv.wait(SY, p0a_red[ci - 2])
            if kind == "gu":
                a = width // 512
                dst = bass.AP(raw[ci % 2], 0, [[RAWW, P], [512, a], [1, 512]])
                src = bass.AP(w_t, off, [[512, P], [P * 512, a], [1, 512]])
            else:
                dst = bass.AP(raw[ci % 2], 0, [[RAWW, P], [1, width]])
                src = bass.AP(w_t, off, [[I, P], [1, width]])
            c = pa_[ci % 2]
            SY.dma_start(out=dst, in_=src).then_inc(c.h, 16); c.n += 16
            V.wait_ge(c.h, c.n)
            p0a_red[ci] = V.tensor_reduce(
                colsum[:, ci:ci + 1],
                bass.AP(raw[ci % 2], 0, [[RAWW, P], [1, width]]),
                AX.X, ALU.add, apply_absolute_value=True, _ctr=vv)

    inv_h = 1.0 / float(H)
    yq_copy = {}
    x1_dma = {}

    for tau in range(TT):
        m_mean, m_neg, m_var, m_aeps, m_sig, m_t, m_u, m_s1, m_rstd, m_x = sm
        xt = xa[tau % 2]

        def ln_stats(src):
            xc, xv = xload[tau]
            V.wait_ge(xc.h, xv)
            V.tensor_reduce(m_mean[:, :], src[:, :], AX.X, ALU.add)
            V.tensor_scalar(m_mean[:, :], m_mean[:, :], inv_h, None, ALU.mult)
            v1 = V.tensor_scalar(m_neg[:, :], m_mean[:, :], -1.0, None, ALU.mult, _ctr=vv)
            vv.wait(S, v1)
            S.activation(xcb[:, :], src[:, :], ACT.Identity, bias=m_neg[:, :], scale=1.0)
            s1 = S.activation(f16t[:, :], xcb[:, :], ACT.Square, accum_out=m_var[:, :], _ctr=ss)
            ss.wait(V, s1)
            V.tensor_scalar(m_var[:, :], m_var[:, :], inv_h, None, ALU.mult)
            v2 = V.tensor_scalar(m_aeps[:, :], m_var[:, :], 1e-5, None, ALU.add, _ctr=vv)
            vv.wait(S, v2)
            s2 = S.activation(m_sig[:, :], m_aeps[:, :], ACT.Sqrt, _ctr=ss)
            ss.wait(V, s2)
            V.reciprocal(m_t[:, :], m_sig[:, :])
            V.tensor_tensor(m_t[:, :], m_aeps[:, :], m_t[:, :], ALU.mult)
            V.tensor_tensor(m_u[:, :], m_sig[:, :], m_t[:, :], ALU.add)
            V.tensor_scalar(m_s1[:, :], m_u[:, :], 0.5, None, ALU.mult)
            V.reciprocal(m_rstd[:, :], m_s1[:, :])

        ln_stats(xt)
        V.tensor_scalar(m_x[:, :], m_rstd[:, :], 2.0, None, ALU.mult)
        v4 = V.scalar_tensor_tensor(x1b[:, :], xcb[:, :], m_x[:, :], xt[:, :],
                                    ALU.mult, ALU.add, _ctr=vv)
        vv.wait(S, v4)
        S.dma_start(out=bass.AP(x1_d, tau * P * H, [[H, P], [1, H]]),
                    in_=x1b[:, :]).then_inc(x1st.h, 16); x1st.n += 16
        x1_dma[tau] = x1st.n
        if tau + 2 < TT:
            load_x(tau + 2)

        ln_stats(x1b)
        V.tensor_reduce(m_t[:, :], xcb[:, :], AX.X, ALU.max, apply_absolute_value=True)
        V.tensor_scalar(m_u[:, :], m_t[:, :], m_rstd[:, :], 1e-5, ALU.mult, ALU.max)
        V.tensor_tensor(m_t[:, :], m_u[:, :], m_u[:, :], ALU.mult)
        V.tensor_tensor(c3b[:, tau:tau + 1], m_t[:, :], m_u[:, :], ALU.mult)
        V.reciprocal(m_t[:, :], m_u[:, :])
        V.tensor_scalar(m_t[:, :], m_t[:, :], 127.0, None, ALU.mult)
        v5 = V.tensor_tensor(m_x[:, :], m_t[:, :], m_rstd[:, :], ALU.mult, _ctr=vv)
        vv.wait(S, v5)
        x1st.wait(S, x1_dma[tau])
        S.activation(x1b[:, :], xcb[:, :], ACT.Identity, bias=c1536[:, :], scale=m_x[:, :])
        s3 = S.copy(f16t[:, :], x1b[:, :], _ctr=ss)
        ss.wait(V, s3)
        v6 = V.tensor_scalar(yqb[:, :], f16t[:, :], -1536.0, None, ALU.add, _ctr=vv)

        for b in range(KT // 4):
            if tau == 0 and b == 0:
                cst.wait(Tn, cst.n)
            vv.wait(Tn, v6)
            if (tau * (KT // 4) + b) >= 2:
                key = (tau * (KT // 4) + b) - 2
                vv.wait(Tn, yq_copy[key])
            for j in range(4):
                ins = Tn.matmul(pstY[b % 2][:, j * P:(j + 1) * P],
                                yqb[:, (b * 4 + j) * P:(b * 4 + j + 1) * P],
                                identb[:, :], is_transpose=True,
                                start=(j == 0), stop=(j == 3))
            tb_ = tt.inc(ins)
            tt.wait(V, tb_)
            yq_copy[tau * (KT // 4) + b] = V.tensor_copy(
                bass.AP(Y, (b * 4) * T + tau * P, [[KT * T, P], [T, 4], [1, P]]),
                pstY[b % 2][:, 0:512], _ctr=vv)

        emit_p0a(10)

    emit_p0a(len(p0a_chunks))
    v_Yfin = vv.n

    if phase_limit == 'pa':
        x1st.wait(SY, x1st.n)
        for tau in range(TT):
            SY.dma_start(out=bass.AP(out_d, tau * P * H, [[H, P], [1, H]]),
                         in_=bass.AP(x1_d, tau * P * H, [[H, P], [1, H]]))\
              .then_inc(du_.h, 16); du_.n += 16
        SY.wait_ge(du_.h, du_.n)
        S.wait_ge(du_.h, du_.n)
        stA.close()
        stY.close()
        st.close()
        return nc

    nch_g = (I + 511) // 512
    nd0 = 2 * nch_g
    V.tensor_reduce(totb[:, 0:1], colsum[:, 0:nch_g], AX.X, ALU.add)
    V.tensor_reduce(totb[:, 1:2], colsum[:, nch_g:2 * nch_g], AX.X, ALU.add)
    v_tot = V.tensor_reduce(totb[:, 2:3], colsum[:, nd0:p0a_next[0]], AX.X, ALU.add, _ctr=vv)
    vv.wait(Tn, v_tot)
    t_tot = tt.inc(Tn.transpose(ptot[:, :], totb[:, :], identf[:, :]))
    tt.wait(V, t_tot)
    v_sums = V.tensor_reduce(sums4[:, :], ptot[:, :], AX.X, ALU.add, _ctr=vv)
    vv.wait(G, v_sums)
    G.dma_start(out=bass.AP(ar_in, 0, [[4, 1], [1, 4]]),
                in_=bass.AP(sums4, 0, [[1, 4], [1, 1]]),
                allow_slow_non_contiguous=True).then_inc(gd.h, 16); gd.n += 16
    G.wait_ge(gd.h, gd.n)
    arsem = new_coll_sem("ar")
    arsem.inc(G.collective_compute("AllReduce", ALU.add,
                                   replica_groups=[list(range(NC))],
                                   ins=[ar_in.ap().opt()], outs=[ar_out.ap().opt()]))
    G.wait_ge(arsem.h, arsem.n)
    G.dma_start(out=arrow[:, :], in_=ar_out[:, :]).then_inc(gd.h, 16); gd.n += 16
    gd_ar = gd.n

    gd.wait(V, gd_ar)
    V.memset(rowb[:, 0:1], NTOT)
    V.memset(rowb[:, 1:2], 127.0)
    V.reciprocal(rowb[:, 2:3], rowb[:, 0:1])
    V.reciprocal(rowb[:, 3:4], rowb[:, 1:2])
    for m in range(3):
        V.tensor_tensor(meansb[:, m:m + 1], arrow[:, m:m + 1], rowb[:, 2:3], ALU.mult)
        V.tensor_scalar(rowb[:, 8 + m:9 + m], meansb[:, m:m + 1], 1e-5, None, ALU.max)
        V.reciprocal(rowb[:, 4 + m:5 + m], rowb[:, 8 + m:9 + m])
    V.tensor_tensor(rowb[:, 8:9], meansb[:, 0:1], meansb[:, 0:1], ALU.mult)
    V.tensor_tensor(rowb[:, 8:9], rowb[:, 8:9], meansb[:, 1:2], ALU.mult)
    V.tensor_tensor(rowb[:, 9:10], rowb[:, 3:4], rowb[:, 3:4], ALU.mult)
    V.tensor_tensor(rowb[:, 9:10], rowb[:, 9:10], rowb[:, 3:4], ALU.mult)
    V.tensor_tensor(rowb[:, 7:8], rowb[:, 8:9], rowb[:, 9:10], ALU.mult)
    V.tensor_tensor(rowb[:, 8:9], meansb[:, 2:3], rowb[:, 3:4], ALU.mult)
    V.tensor_copy(rowb[:, 0:1], rowb[:, 4:5])
    V.tensor_copy(rowb[:, 1:2], rowb[:, 5:6])
    V.tensor_copy(rowb[:, 2:3], rowb[:, 6:7])
    V.tensor_copy(rowb[:, 3:4], rowb[:, 7:8])
    v_row = V.tensor_copy(rowb[:, 4:5], rowb[:, 8:9], _ctr=vv)
    vv.wait(Tn, v_row)
    t_rep = tt.inc(Tn.matmul(prep8[:, :], onesf[:, :], rowb[:, 0:8], start=True, stop=True))
    tt.wait(V, t_rep)
    v_rep = V.tensor_copy(repscal[:, :], prep8[:, :], _ctr=vv)

    wuse = [0]
    stg_dma = {}
    stg_copy = {}

    def emit_tile_store(src_sb, src_w, src_off, ag_t, dst_off, agwidth, v_src):
        w = wuse[0]; wuse[0] += 1
        vv.wait(Tn, v_src)
        if w >= 2:
            vv.wait(Tn, stg_copy[w - 2])
        for j in range(4):
            ins = Tn.matmul(pstW[w % 2][:, j * P:(j + 1) * P],
                            bass.AP(src_sb, src_off + j * P, [[src_w, P], [1, P]]),
                            identb[:, :], is_transpose=True,
                            start=(j == 0), stop=(j == 3))
        tw = tt.inc(ins)
        tt.wait(V, tw)
        if w >= 2:
            sc, sv = stg_dma[w - 2]
            V.wait_ge(sc.h, sv)
        stg_copy[w] = V.tensor_copy(stgW[w % 2][:, :], pstW[w % 2][:, 0:512], _ctr=vv)
        vv.wait(S, stg_copy[w])
        c = stg_[w % 2]
        S.dma_start(out=bass.AP(ag_t, dst_off, [[agwidth, P], [1, 512]]),
                    in_=stgW[w % 2][:, :]).then_inc(c.h, 16); c.n += 16
        stg_dma[w] = (c, c.n)

    def publish_group_stores():
        S.wait_ge(stg_[0].h, stg_[0].n)
        S.wait_ge(stg_[1].h, stg_[1].n)
        S.activation(sm[1][:, :], sm[0][:, :], ACT.Identity, _ctr=gst)
        return gst.n

    rawuse = [0]
    tern_done = {}

    p0a_last_red = max(p0a_red.values()) if p0a_red else 0

    def p0b_gu(w_t, scal_col, ag_t, s0, sz):
        agw = sz * 512
        for mt0 in range(s0, s0 + sz, 2):
            cnt = min(2, s0 + sz - mt0)
            ru = rawuse[0]; rawuse[0] += 1
            if ru >= 2:
                SY.wait_ge(vv.h, tern_done[ru - 2])
            else:
                vv.wait(SY, p0a_last_red)
            c = pb_[ru % 2]
            SY.dma_start(out=bass.AP(raw[ru % 2], 0, [[RAWW, P], [512, cnt], [1, 512]]),
                         in_=bass.AP(w_t, mt0 * P * 512, [[512, P], [P * 512, cnt], [1, 512]]))\
              .then_inc(c.h, 16); c.n += 16
            V.wait_ge(c.h, c.n)
            vv.wait(V, v_rep)
            w = cnt * 512
            V.tensor_scalar(pb[:, 0:w], bass.AP(raw[ru % 2], 0, [[RAWW, P], [1, w]]),
                            repscal[:, scal_col:scal_col + 1], 0.5, ALU.mult, ALU.is_gt)
            V.tensor_scalar(nb[:, 0:w], bass.AP(raw[ru % 2], 0, [[RAWW, P], [1, w]]),
                            repscal[:, scal_col:scal_col + 1], -0.5, ALU.mult, ALU.is_lt)
            tern_done[ru] = V.tensor_tensor(ternb[:, 0:w], pb[:, 0:w], nb[:, 0:w],
                                            ALU.subtract, _ctr=vv)
            for k in range(cnt):
                emit_tile_store(ternb, 1024, k * 512, ag_t, (mt0 - s0 + k) * 512, agw, tern_done[ru])

    cg_g, cg_u = {}, {}
    for g, (s0, sz) in enumerate(ggrp):
        p0b_gu(wg_in, 0, agg_in[g], s0, sz)
        p0b_gu(wu_in, 1, agu_in[g], s0, sz)
        gp = publish_group_stores()
        gst.wait(G, gp)
        cg_g[g] = new_coll_sem(f"g{g}")
        cg_g[g].inc(G.collective_compute("AllGather", ALU.bypass,
                                         replica_groups=[list(range(NC))],
                                         ins=[agg_in[g].ap().opt()], outs=[agg_out[g].ap().opt()]))
        cg_u[g] = new_coll_sem(f"u{g}")
        cg_u[g].inc(G.collective_compute("AllGather", ALU.bypass,
                                         replica_groups=[list(range(NC))],
                                         ins=[agu_in[g].ap().opt()], outs=[agu_out[g].ap().opt()]))

    ternD_prev = [0]
    cg_d = {}
    for gdi, (s0, sz) in enumerate(dgrp):
        agw = sz * 512
        for kt0 in range(s0, s0 + sz, 4):
            cnt = min(4, s0 + sz - kt0)
            if ternD_prev[0]:
                vv.wait(SY, ternD_prev[0])
            for hg in range(4):
                SY.dma_start(out=bass.AP(rawD[hg], 0, [[512, P], [1, cnt * P]]),
                             in_=bass.AP(wd_in, hg * P * I + kt0 * P, [[I, P], [1, cnt * P]]))\
                  .then_inc(pd_s.h, 16); pd_s.n += 16
            pd_s.wait(V, pd_s.n)
            vv.wait(V, v_rep)
            for hg in range(4):
                w = cnt * P
                V.tensor_scalar(pb[:, 0:w], rawD[hg][:, 0:w],
                                repscal[:, 2:3], 0.5, ALU.mult, ALU.is_gt)
                V.tensor_scalar(nb[:, 0:w], rawD[hg][:, 0:w],
                                repscal[:, 2:3], -0.5, ALU.mult, ALU.is_lt)
                last = V.tensor_tensor(ternD[hg][:, 0:w], pb[:, 0:w], nb[:, 0:w],
                                       ALU.subtract, _ctr=(vv if hg == 3 else None))
            ternD_prev[0] = last
            for k in range(cnt):
                w = wuse[0]; wuse[0] += 1
                vv.wait(Tn, ternD_prev[0])
                if w >= 2:
                    vv.wait(Tn, stg_copy[w - 2])
                for ml in range(4):
                    ins = Tn.matmul(pstW[w % 2][:, ml * P:(ml + 1) * P],
                                    ternD[ml][:, k * P:(k + 1) * P], identb[:, :],
                                    is_transpose=True, start=(ml == 0), stop=(ml == 3))
                tw = tt.inc(ins)
                tt.wait(V, tw)
                if w >= 2:
                    sc, sv = stg_dma[w - 2]
                    V.wait_ge(sc.h, sv)
                stg_copy[w] = V.tensor_copy(stgW[w % 2][:, :], pstW[w % 2][:, 0:512], _ctr=vv)
                vv.wait(S, stg_copy[w])
                c = stg_[w % 2]
                S.dma_start(out=bass.AP(agd_in[gdi], (kt0 - s0 + k) * 512, [[agw, P], [1, 512]]),
                            in_=stgW[w % 2][:, :]).then_inc(c.h, 16); c.n += 16
                stg_dma[w] = (c, c.n)
        gp = publish_group_stores()
        gst.wait(G, gp)
        cg_d[gdi] = new_coll_sem(f"d{gdi}")
        cg_d[gdi].inc(G.collective_compute("AllGather", ALU.bypass,
                                           replica_groups=[list(range(NC))],
                                           ins=[agd_in[gdi].ap().opt()], outs=[agd_out[gdi].ap().opt()]))

    v_p0b = vv.n
    t_p0b = tt.n
    g_p0b = gst.n

    def _dummy_out():
        x1st.wait(SY, x1st.n)
        for tau in range(TT):
            SY.dma_start(out=bass.AP(out_d, tau * P * H, [[H, P], [1, H]]),
                         in_=bass.AP(x1_d, tau * P * H, [[H, P], [1, H]]))\
              .then_inc(du_.h, 16); du_.n += 16
        SY.wait_ge(du_.h, du_.n)
        S.wait_ge(du_.h, du_.n)
        for cs in csem.values():
            G.wait_ge(cs.h, cs.n)
        G.wait_ge(gd.h, gd.n)

    if phase_limit == 'p0':
        _dummy_out()
        if dbg:
            for cs in csem.values():
                SY.wait_ge(cs.h, cs.n)
            for fam, outs, grp in (("dmpg", agg_out, ggrp), ("dmpu", agu_out, ggrp),
                                   ("dmpd", agd_out, dgrp)):
                for g_, (s_, sz_) in enumerate(grp):
                    td = nc.dram_tensor(f"{fam}{g_}", [P * NC, sz_ * 512], BF16,
                                        kind="ExternalOutput")
                    SY.dma_start(out=td[:, :], in_=outs[g_][:, :]).then_inc(du_.h, 16)
                    du_.n += 16
            for fam, ins_, grp in (("dmpig", agg_in, ggrp), ("dmpiu", agu_in, ggrp),
                                   ("dmpid", agd_in, dgrp)):
                for g_, (s_, sz_) in enumerate(grp):
                    td = nc.dram_tensor(f"{fam}{g_}", [P, sz_ * 512], BF16,
                                        kind="ExternalOutput")
                    SY.dma_start(out=td[:, :], in_=ins_[g_][:, :]).then_inc(du_.h, 16)
                    du_.n += 16
            SY.wait_ge(du_.h, du_.n)
        stA.close()
        stY.close()
        st.close()
        return nc

    stA.close()
    stM = ExitStack()
    wbuf = [stM.enter_context(nc.sbuf_tensor(f"wbuf{i}", [P, 2 * KT * P], BF16)) for i in range(2)]
    rgb = [stM.enter_context(nc.sbuf_tensor(f"rgb{i}", [P, T], F32)) for i in range(2)]
    rg2 = [stM.enter_context(nc.sbuf_tensor(f"rg2{i}", [P, T], F32)) for i in range(2)]
    ebuf = [stM.enter_context(nc.sbuf_tensor(f"ebuf{i}", [P, T], F32)) for i in range(2)]
    gps = [stM.enter_context(nc.psum_tensor(f"gps{i}", [P, T], F32)) for i in range(2)]
    ups = [stM.enter_context(nc.psum_tensor(f"ups{i}", [P, T], F32)) for i in range(2)]

    mt_group = {}
    for g, (s0, sz) in enumerate(ggrp):
        for mt in range(s0, s0 + sz):
            mt_group[mt] = g
    wb_dma = {}
    T_MAIN0 = tt.n
    e_dma = {}
    for mt in range(MT):
        g = mt_group[mt]
        s0, sz = ggrp[g]
        SY.wait_ge(cg_g[g].h, cg_g[g].n)
        SY.wait_ge(cg_u[g].h, cg_u[g].n)
        if mt == 0:
            vv.wait(SY, v_p0b)
            tt.wait(SY, t_p0b)
            gst.wait(SY, g_p0b)
        if mt >= 2:
            tt.wait(SY, T_MAIN0 + mt - 1)
        cwb = wb_[mt % 2]
        for ag_t, coff in ((agg_out[g], 0), (agu_out[g], KT * P)):
            SY.dma_start(
                out=bass.AP(wbuf[mt % 2], coff, [[2 * KT * P, P], [512, NC], [1, 512]]),
                in_=bass.AP(ag_t, (mt - s0) * 512, [[sz * 512, P], [P * sz * 512, NC], [1, 512]]))\
              .then_inc(cwb.h, 16); cwb.n += 16
        wb_dma[mt] = (cwb, cwb.n)
        Tn.wait_ge(cwb.h, cwb.n)
        if mt == 0:
            vv.wait(Tn, v_Yfin)
        if mt >= 2:
            se.wait(Tn, mt - 1)
            ve.wait(Tn, mt - 1)
        for kt in range(KT):
            st0, sp0 = (kt == 0), (kt == KT - 1)
            for th in range(NH):
                ins = Tn.matmul(gps[mt % 2][:, th * 512:(th + 1) * 512],
                                wbuf[mt % 2][:, kt * P:(kt + 1) * P],
                                Y[:, kt * T + th * 512:kt * T + (th + 1) * 512],
                                start=st0, stop=sp0)
            for th in range(NH):
                ins = Tn.matmul(ups[mt % 2][:, th * 512:(th + 1) * 512],
                                wbuf[mt % 2][:, KT * P + kt * P:KT * P + (kt + 1) * P],
                                Y[:, kt * T + th * 512:kt * T + (th + 1) * 512],
                                start=st0, stop=sp0)
        tmm = tt.inc(ins)
        tt.wait(S, tmm)
        S.activation(rgb[mt % 2][:, :], gps[mt % 2][:, :], ACT.Relu)
        S.activation(rg2[mt % 2][:, :], rgb[mt % 2][:, :], ACT.Square, _ctr=se)
        se.wait(V, mt + 1)
        if mt >= 2:
            ec, ev = e_dma[mt - 2]
            V.wait_ge(ec.h, ev)
        V.tensor_tensor(ebuf[mt % 2][:, :], rg2[mt % 2][:, :], ups[mt % 2][:, :], ALU.mult)
        V.tensor_tensor(maxbuf2[:, :], ebuf[mt % 2][:, :], maxbuf[:, :], ALU.max)
        V.tensor_tensor(minbuf2[:, :], ebuf[mt % 2][:, :], minbuf[:, :], ALU.min, _ctr=ve)
        maxbuf, maxbuf2 = maxbuf2, maxbuf
        minbuf, minbuf2 = minbuf2, minbuf
        ve.wait(S, mt + 1)
        ce = est_[mt % 2]
        S.dma_start(out=bass.AP(e_d, mt * P * T, [[T, P], [1, T]]),
                    in_=ebuf[mt % 2][:, :]).then_inc(ce.h, 16); ce.n += 16
        e_dma[mt] = (ce, ce.n)

    if phase_limit == 'main':
        _dummy_out()
        stM.close()
        stY.close()
        st.close()
        return nc
    if dbg:
        Y_out = nc.dram_tensor("dbg_Y", [P, KT * T], BF16, kind="ExternalOutput")
        SY.wait_ge(vv.h, v_Yfin)
        SY.dma_start(out=Y_out[:, :], in_=Y[:, :]).then_inc(du_.h, 16); du_.n += 16
    stM.close()
    stY.close()

    psirep = st.enter_context(nc.sbuf_tensor("psirep", [P, T], F32))
    stP = ExitStack()
    pmax = stP.enter_context(nc.psum_tensor("pmax", [P, 512], F32))
    psit = stP.enter_context(nc.psum_tensor("psit", [TT, P], F32))
    prep2 = stP.enter_context(nc.psum_tensor("prep2", [P, 512], F32))
    psiT = stP.enter_context(nc.sbuf_tensor("psiT", [TT, P], F32))
    psirow = stP.enter_context(nc.sbuf_tensor("psirow", [1, T], F32))

    vv.wait(V, v_rep)
    v_gam = V.tensor_scalar(gam[:, :], c3b[:, :], repscal[:, 3:4], None, ALU.mult, _ctr=vv)
    ve.wait(V, MT)
    V.tensor_scalar(minbuf2[:, :], minbuf[:, :], -1.0, None, ALU.mult)
    v_fold = V.tensor_tensor(maxbuf2[:, :], maxbuf[:, :], minbuf2[:, :], ALU.max, _ctr=vv)
    maxbuf = maxbuf2
    last_red = None
    for tau in range(TT):
        vv.wait(Tn, v_fold)
        if last_red is not None:
            vv.wait(Tn, last_red)
        ttr = tt.inc(Tn.transpose(pmax[:, 0:P], maxbuf[:, tau * P:(tau + 1) * P], identf[:, :]))
        tt.wait(V, ttr)
        last_red = V.tensor_reduce(mec[:, tau:tau + 1], pmax[:, 0:P], AX.X, ALU.max, _ctr=vv)
    V.tensor_tensor(tmpa[:, :], gam[:, :], mec[:, :], ALU.mult)
    V.tensor_scalar(tmpa[:, :], tmpa[:, :], 1e-5, None, ALU.max)
    V.tensor_scalar(phic[:, :], tmpa[:, :], repscal[:, 4:5], None, ALU.mult)
    V.reciprocal(tmpb[:, :], tmpa[:, :])
    V.tensor_scalar(tmpb[:, :], tmpb[:, :], 127.0, None, ALU.mult)
    v_psi = V.tensor_tensor(psic[:, :], tmpb[:, :], gam[:, :], ALU.mult, _ctr=vv)
    vv.wait(Tn, v_psi)
    t_pt = tt.inc(Tn.transpose(psit[:, :], psic[:, :], identf[:, :]))
    tt.wait(V, t_pt)
    v_pT = V.tensor_copy(psiT[:, :], psit[:, :], _ctr=vv)
    vv.wait(SY, v_pT)
    SY.dma_start(out=bass.AP(psirow, 0, [[T, 1], [P, TT], [1, P]]),
                 in_=psiT[:, :]).then_inc(pr_s.h, 16); pr_s.n += 16
    v_prep_last = 0
    for j in range(T // 512):
        pr_s.wait(Tn, pr_s.n)
        if v_prep_last:
            vv.wait(Tn, v_prep_last)
        t_r = tt.inc(Tn.matmul(prep2[:, :], onesf[:, :], psirow[:, j * 512:(j + 1) * 512],
                               start=True, stop=True))
        tt.wait(V, t_r)
        v_prep_last = V.tensor_copy(psirep[:, j * 512:(j + 1) * 512], prep2[:, :], _ctr=vv)
    v_psirep = v_prep_last
    stP.close()

    if dbg:
        SY.wait_ge(vv.h, v_psirep)
        for nm, sb, rr, cc_ in (("dbgE_maxbuf", maxbuf, P, T), ("dbgE_mec", mec, P, TT),
                                ("dbgE_gam", gam, P, TT), ("dbgE_phic", phic, P, TT),
                                ("dbgE_psic", psic, P, TT)):
            tdb = nc.dram_tensor(nm, [rr, cc_], F32, kind="ExternalOutput")
            SY.dma_start(out=tdb[:, :], in_=sb[:, :]).then_inc(du_.h, 16); du_.n += 16

    if phase_limit == 'psi':
        vv.wait(SY, v_psirep)
        _dummy_out()
        st.close()
        return nc

    stD = ExitStack()
    Hq = stD.enter_context(nc.sbuf_tensor("Hq", [P, MT * 512], BF16))
    etile = [stD.enter_context(nc.sbuf_tensor(f"etile{i}", [P, 512], F32)) for i in range(2)]
    vb2 = stD.enter_context(nc.sbuf_tensor("vb2", [P, 512], F32))
    fb2 = stD.enter_context(nc.sbuf_tensor("fb2", [P, 512], F16))
    wdbW = max(sz for _, sz in dgrp) * 512
    wdb = [stD.enter_context(nc.sbuf_tensor(f"wdb{i}", [P, wdbW], BF16)) for i in range(2)]
    x1s = [[stD.enter_context(nc.sbuf_tensor(f"x1s{i}_{t}", [P, 512], F32)) for t in range(TG)]
           for i in range(2)]
    outb = [stD.enter_context(nc.sbuf_tensor(f"outb{i}", [P, TG * 512], F32)) for i in range(2)]
    dps = [stD.enter_context(nc.psum_tensor(f"dps{i}", [P, TG * 512], F32)) for i in range(2)]

    est_[0].wait(SY, est_[0].n)
    est_[1].wait(SY, est_[1].n)
    x1st.wait(SY, x1st.n)
    n_round = 0
    out_dma = {}
    t3_blocks = []
    for g2 in range(NTG):
        for kt in range(MT):
            if kt >= 2:
                vq.wait(SY, g2 * MT + kt - 1)
            cet = et_[kt % 2]
            SY.dma_start(out=etile[kt % 2][:, :],
                         in_=bass.AP(e_d, kt * P * T + g2 * TG * P, [[T, P], [1, 512]]))\
              .then_inc(cet.h, 16); cet.n += 16
            if g2 == 1:
                for hi, cnt_ in t3_blocks:
                    if kt <= hi:
                        t4.wait(V, cnt_)
                        break
            V.wait_ge(cet.h, cet.n)
            if g2 == 0 and kt == 0:
                vv.wait(V, v_psirep)
            V.tensor_tensor(vb2[:, :], etile[kt % 2][:, :],
                            psirep[:, g2 * TG * P:(g2 * TG + TG) * P], ALU.mult)
            V.tensor_scalar(vb2[:, :], vb2[:, :], 1536.0, None, ALU.add)
            V.tensor_copy(fb2[:, :], vb2[:, :])
            V.tensor_scalar(Hq[:, kt * 512:(kt + 1) * 512], fb2[:, :],
                            -1536.0, None, ALU.add, _ctr=vq)
        for hq in range(HQ):
            par = n_round % 2
            if n_round >= 2:
                vd.wait(SY, n_round - 1)
            cx1 = xd_[par]
            for tl in range(TG):
                SY.dma_start(out=x1s[par][tl][:, :],
                             in_=bass.AP(x1_d, (g2 * TG + tl) * P * H + hq * 512,
                                         [[H, P], [1, 512]]))\
                  .then_inc(cx1.h, 16); cx1.n += 16
            x1_cnt = cx1.n
            if n_round >= 2:
                vd.wait(Tn, n_round - 1)
            for gdi, (s0, sz) in enumerate(dgrp):
                seg = n_round * NGD + gdi
                wpar = seg % 2
                SY.wait_ge(cg_d[gdi].h, cg_d[gdi].n)
                if seg >= 2:
                    t4.wait(SY, seg - 1)
                cwd = wd_[wpar]
                SY.dma_start(out=bass.AP(wdb[wpar], 0, [[wdbW, P], [1, sz * 512]]),
                             in_=bass.AP(agd_out[gdi], hq * P * sz * 512,
                                         [[sz * 512, P], [1, sz * 512]]))\
                  .then_inc(cwd.h, 16); cwd.n += 16
                Tn.wait_ge(cwd.h, cwd.n)
                for k in range(sz):
                    kt = s0 + k
                    vq.wait(Tn, g2 * MT + kt + 1)
                    for tl in range(TG):
                        ins = Tn.matmul(dps[par][:, tl * 512:(tl + 1) * 512],
                                        Hq[:, kt * 512 + tl * P:kt * 512 + (tl + 1) * P],
                                        wdb[wpar][:, k * 512:(k + 1) * 512],
                                        start=(kt == 0), stop=(kt == MT - 1))
                t4.inc(ins)
                if g2 == 0 and hq == HQ - 1:
                    t3_blocks.append((s0 + sz - 1, t4.n))
            t4.wait(V, (n_round + 1) * NGD)
            V.wait_ge(cx1.h, x1_cnt)
            if n_round >= 2:
                oc, ov = out_dma[n_round - 2]
                V.wait_ge(oc.h, ov)
            for tl in range(TG):
                v_dr = V.scalar_tensor_tensor(
                    outb[par][:, tl * 512:(tl + 1) * 512],
                    dps[par][:, tl * 512:(tl + 1) * 512],
                    phic[:, g2 * TG + tl:g2 * TG + tl + 1],
                    x1s[par][tl][:, :], ALU.mult, ALU.add,
                    _ctr=(vd if tl == TG - 1 else None))
            vd.wait(S, v_dr)
            cos = os_[par]
            for tl in range(TG):
                S.dma_start(out=bass.AP(out_d, (g2 * TG + tl) * P * H + hq * 512,
                                        [[H, P], [1, 512]]),
                            in_=outb[par][:, tl * 512:(tl + 1) * 512])\
                  .then_inc(cos.h, 16); cos.n += 16
            out_dma[n_round] = (cos, cos.n)
            n_round += 1

    stD.close()

    if dbg:
        SY.wait_ge(vd.h, n_round)
        for nm, sb, rr, cc_ in (("dbg_maxbuf", maxbuf, P, T), ("dbg_mec", mec, P, TT),
                                ("dbg_gam", gam, P, TT), ("dbg_phic", phic, P, TT),
                                ("dbg_psic", psic, P, TT), ("dbg_rep", repscal, P, 8),
                                ("dbg_psirep", psirep, P, T)):
            tdb = nc.dram_tensor(nm, [rr, cc_], F32, kind="ExternalOutput")
            SY.dma_start(out=tdb[:, :], in_=sb[:, :]).then_inc(du_.h, 16); du_.n += 16

    S.wait_ge(os_[0].h, os_[0].n)
    S.wait_ge(os_[1].h, os_[1].n)
    SY.wait_ge(os_[0].h, os_[0].n)
    SY.wait_ge(os_[1].h, os_[1].n)
    if du_.n:
        SY.wait_ge(du_.h, du_.n)
    for cs in csem.values():
        G.wait_ge(cs.h, cs.n)
    G.wait_ge(gd.h, gd.n)

    st.close()
    return nc


def kernel(L0_attn_x, L0_ffn_w_gate, L0_ffn_w_up, L0_ffn_w_down):
    x = np.asarray(L0_attn_x, dtype=np.float32)
    wg = np.asarray(L0_ffn_w_gate, dtype=np.float32)
    wu = np.asarray(L0_ffn_w_up, dtype=np.float32)
    wd = np.asarray(L0_ffn_w_down, dtype=np.float32)
    B, Seq, Hm = x.shape
    I = wg.shape[0]
    T = (B * Seq) // NC
    nc = build_nc(I=I, T=T, NG=8, NGD=8)
    identb = np.eye(P, dtype=ml_dtypes.bfloat16)
    identf = np.eye(P, dtype=np.float32)
    ones = np.ones((1, P), np.float32)
    x2 = x.reshape(B * Seq, Hm)
    hs = Hm // NC
    in_maps = []
    for r in range(NC):
        in_maps.append({
            "x_in": np.ascontiguousarray(x2[r * T:(r + 1) * T]),
            "wg_in": np.ascontiguousarray(wg[:, r * hs:(r + 1) * hs]),
            "wu_in": np.ascontiguousarray(wu[:, r * hs:(r + 1) * hs]),
            "wd_in": np.ascontiguousarray(wd[r * hs:(r + 1) * hs, :]),
            "identb_in": identb, "identf_in": identf, "ones_in": ones,
        })
    res = run_bass_kernel_spmd(nc, in_maps, core_ids=list(range(NC)))
    out = np.concatenate([np.asarray(res.results[r]["out"]) for r in range(NC)], axis=0)
    return out.reshape(B, Seq, Hm).astype(np.float32)


# version 10
# speedup vs baseline: 1.2228x; 1.0478x over previous
import numpy as np
import ml_dtypes
from contextlib import ExitStack

import concourse.bass as bass
import concourse.mybir as mybir
from concourse.bass_utils import run_bass_kernel_spmd

F32 = mybir.dt.float32
BF16 = mybir.dt.bfloat16
F16 = mybir.dt.float16
ALU = mybir.AluOpType
ACT = mybir.ActivationFunctionType
AX = mybir.AxisListType

H = 4096
NC = 8
P = 128


def _groups(n, g):
    base, rem = divmod(n, g)
    sizes = [base + (1 if i < rem else 0) for i in range(g)]
    starts = [int(x) for x in np.cumsum([0] + sizes[:-1])]
    return list(zip(starts, sizes))


class Ctr:
    def __init__(self, st, nc, name):
        self.h = st.enter_context(nc.semaphore(name))
        self.n = 0

    def inc(self, instr, k=1):
        instr.then_inc(self.h, k)
        self.n += k
        return self.n

    def wait(self, eng, v):
        assert 0 <= v <= self.n, f"{self.h}: wait on future count {v} > {self.n}"
        if v > 0:
            eng.wait_ge(self.h, v)


class SerEng:

    COMPUTE = frozenset({
        "memset", "memzero", "tensor_copy", "tensor_tensor", "tensor_add",
        "tensor_sub", "tensor_mul", "tensor_max", "tensor_scalar",
        "tensor_single_scalar", "tensor_scalar_add", "tensor_scalar_sub",
        "tensor_scalar_mul", "tensor_scalar_max", "tensor_scalar_min",
        "tensor_relu", "tensor_reduce", "reduce_max", "reduce_sum",
        "scalar_tensor_tensor", "reciprocal", "transpose", "bn_stats",
        "bn_aggr", "activation", "copy", "add", "mul", "square", "sqrt",
        "sign", "select",
    })

    def __init__(self, eng, priv):
        self._e = eng
        self._p = priv
        self._last = None

    def __getattr__(self, name):
        attr = getattr(self._e, name)
        if name not in SerEng.COMPUTE:
            return attr

        def shim(*a, _ctr=None, **kw):
            if self._last is not None:
                c, v = self._last
                c.wait(self._e, v)
            ins = attr(*a, **kw)
            c = _ctr if _ctr is not None else self._p
            v = c.inc(ins)
            self._last = (c, v)
            return v
        return shim


def build_nc(I=11008, T=1024, NG=8, NGD=8, phase_limit='full', dbg=False):
    KT = H // P
    MT = I // P
    TT = T // P
    TG = 4 if TT >= 8 else TT
    NTG = TT // TG
    HQ = H // 512
    NH = T // 512
    ggrp = _groups(MT, NG)
    dgrp = _groups(MT, NGD)
    NTOT = float(I) * float(H)

    nc = bass.Bass("TRN2", target_bir_lowering=False, debug=False,
                   detect_race_conditions=False)

    x_in = nc.dram_tensor("x_in", [T, H], F32, kind="ExternalInput")
    wg_in = nc.dram_tensor("wg_in", [I, 512], F32, kind="ExternalInput")
    wu_in = nc.dram_tensor("wu_in", [I, 512], F32, kind="ExternalInput")
    wd_in = nc.dram_tensor("wd_in", [512, I], F32, kind="ExternalInput")
    identb_in = nc.dram_tensor("identb_in", [P, P], BF16, kind="ExternalInput")
    identf_in = nc.dram_tensor("identf_in", [P, P], F32, kind="ExternalInput")
    ones_in = nc.dram_tensor("ones_in", [1, P], F32, kind="ExternalInput")
    out_d = nc.dram_tensor("out", [T, H], F32, kind="ExternalOutput")

    ar_in = nc.dram_tensor("ar_in", [1, 4], F32)
    ar_out = nc.dram_tensor("ar_out", [1, 4], F32)
    agg_in = [nc.dram_tensor(f"agg_in{g}", [P, sz * 512], BF16) for g, (s, sz) in enumerate(ggrp)]
    agg_out = [nc.dram_tensor(f"agg_out{g}", [P * NC, sz * 512], BF16, addr_space="Shared") for g, (s, sz) in enumerate(ggrp)]
    agu_in = [nc.dram_tensor(f"agu_in{g}", [P, sz * 512], BF16) for g, (s, sz) in enumerate(ggrp)]
    agu_out = [nc.dram_tensor(f"agu_out{g}", [P * NC, sz * 512], BF16, addr_space="Shared") for g, (s, sz) in enumerate(ggrp)]
    agd_in = [nc.dram_tensor(f"agd_in{g}", [P, sz * 512], BF16) for g, (s, sz) in enumerate(dgrp)]
    agd_out = [nc.dram_tensor(f"agd_out{g}", [P * NC, sz * 512], BF16, addr_space="Shared") for g, (s, sz) in enumerate(dgrp)]
    x1_d = nc.dram_tensor("x1_d", [T, H], F32,
                          kind="ExternalOutput" if dbg else "Internal")
    e_d = nc.dram_tensor("e_d", [MT * P, T], F32,
                         kind="ExternalOutput" if dbg else "Internal")

    st = ExitStack()
    gd = Ctr(st, nc, "gd")
    cst = Ctr(st, nc, "cst")
    xs_ = [Ctr(st, nc, f"xs{i}") for i in range(2)]
    pa_ = [Ctr(st, nc, f"pa{i}") for i in range(2)]
    x1st = Ctr(st, nc, "x1st")
    stg_ = [Ctr(st, nc, f"stg{i}") for i in range(4)]
    gst = Ctr(st, nc, "gst")
    pb_ = [Ctr(st, nc, f"pbs{i}") for i in range(4)]
    pd_ = [Ctr(st, nc, f"pds{i}") for i in range(2)]
    wb_ = [Ctr(st, nc, f"wbs{i}") for i in range(2)]
    est_ = [Ctr(st, nc, f"est{i}") for i in range(2)]
    et_ = [Ctr(st, nc, f"ets{i}") for i in range(2)]
    xd_ = [Ctr(st, nc, f"xds{i}") for i in range(2)]
    wd_ = [Ctr(st, nc, f"wds{i}") for i in range(2)]
    os_ = [Ctr(st, nc, f"oss{i}") for i in range(2)]
    pr_s = Ctr(st, nc, "prs")
    du_ = Ctr(st, nc, "dus")
    vv = Ctr(st, nc, "vv")
    ss = Ctr(st, nc, "ss")
    scp = Ctr(st, nc, "scp")
    tt = Ctr(st, nc, "tt")
    se = Ctr(st, nc, "se")
    ve = Ctr(st, nc, "ve")
    t4 = Ctr(st, nc, "t4")
    vq = Ctr(st, nc, "vq")
    vd = Ctr(st, nc, "vd")
    vpc = Ctr(st, nc, "vpc")
    spc = Ctr(st, nc, "spc")
    csem = {}

    def new_coll_sem(name):
        csem[name] = Ctr(st, nc, f"cl_{name}")
        return csem[name]

    identb = st.enter_context(nc.sbuf_tensor("identb", [P, P], BF16))
    identf = st.enter_context(nc.sbuf_tensor("identf", [P, P], F32))
    onesf = st.enter_context(nc.sbuf_tensor("onesf", [1, P], F32))
    repscal = st.enter_context(nc.sbuf_tensor("repscal", [P, 8], F32))
    maxbuf = st.enter_context(nc.sbuf_tensor("maxbuf", [P, T], F32))
    minbuf = st.enter_context(nc.sbuf_tensor("minbuf", [P, T], F32))
    maxbuf2 = st.enter_context(nc.sbuf_tensor("maxbuf2", [P, T], F32))
    minbuf2 = st.enter_context(nc.sbuf_tensor("minbuf2", [P, T], F32))
    c3b = st.enter_context(nc.sbuf_tensor("c3b", [P, TT], F32))
    gam = st.enter_context(nc.sbuf_tensor("gam", [P, TT], F32))
    mec = st.enter_context(nc.sbuf_tensor("mec", [P, TT], F32))
    phic = st.enter_context(nc.sbuf_tensor("phic", [P, TT], F32))
    psic = st.enter_context(nc.sbuf_tensor("psic", [P, TT], F32))
    tmpa = st.enter_context(nc.sbuf_tensor("tmpa", [P, TT], F32))
    tmpb = st.enter_context(nc.sbuf_tensor("tmpb", [P, TT], F32))
    sm = [st.enter_context(nc.sbuf_tensor(f"sm{i}", [P, 1], F32)) for i in range(10)]
    c1536 = st.enter_context(nc.sbuf_tensor("c1536", [P, 1], F32))
    rowb = st.enter_context(nc.sbuf_tensor("rowb", [1, 16], F32))
    meansb = st.enter_context(nc.sbuf_tensor("meansb", [1, 4], F32))
    colsum = st.enter_context(nc.sbuf_tensor("colsum", [P, 160], F32))
    totb = st.enter_context(nc.sbuf_tensor("totb", [P, 4], F32))
    sums4 = st.enter_context(nc.sbuf_tensor("sums4", [4, 1], F32))
    arrow = st.enter_context(nc.sbuf_tensor("arrow", [1, 4], F32))

    stY = ExitStack()
    Y = stY.enter_context(nc.sbuf_tensor("Y", [P, KT * T], BF16))

    stA = ExitStack()
    xa = [stA.enter_context(nc.sbuf_tensor(f"xa{i}", [P, H], F32)) for i in range(2)]
    xcb = stA.enter_context(nc.sbuf_tensor("xcb", [P, H], F32))
    x1b = stA.enter_context(nc.sbuf_tensor("x1b", [P, H], F32))
    f16t = stA.enter_context(nc.sbuf_tensor("f16t", [P, H], F16))
    yqb = stA.enter_context(nc.sbuf_tensor("yqb", [P, H], BF16))
    RAWW = 1024
    raw = [stA.enter_context(nc.sbuf_tensor(f"raw{i}", [P, RAWW], F32)) for i in range(4)]
    pb = stA.enter_context(nc.sbuf_tensor("pb", [P, 512], F32))
    nb = stA.enter_context(nc.sbuf_tensor("nb", [P, 512], F32))
    ternb = [stA.enter_context(nc.sbuf_tensor(f"ternb{i}", [P, 512], BF16)) for i in range(2)]
    rawD = [[stA.enter_context(nc.sbuf_tensor(f"rawD{b}_{i}", [P, 256], F32)) for i in range(4)]
            for b in range(2)]
    ternD = [[stA.enter_context(nc.sbuf_tensor(f"ternD{b}_{i}", [P, 256], BF16)) for i in range(4)]
             for b in range(2)]
    stgW = [stA.enter_context(nc.sbuf_tensor(f"stgW{i}", [P, 512], BF16)) for i in range(4)]
    pstY = [stA.enter_context(nc.psum_tensor(f"pstY{i}", [P, 1024], BF16)) for i in range(2)]
    pstW = [stA.enter_context(nc.psum_tensor(f"pstW{i}", [P, 1024], BF16)) for i in range(2)]
    ptot = stA.enter_context(nc.psum_tensor("ptot", [4, P], F32))
    prep8 = stA.enter_context(nc.psum_tensor("prep8", [P, 8], F32))

    blk = st.enter_context(nc.Block())
    Tn, G, SY = nc.tensor, nc.gpsimd, nc.sync
    V = SerEng(nc.vector, vpc)
    S = SerEng(nc.scalar, spc)

    SY.dma_start(out=identb[:, :], in_=identb_in[:, :]).then_inc(cst.h, 16); cst.n += 16
    SY.dma_start(out=identf[:, :], in_=identf_in[:, :]).then_inc(cst.h, 16); cst.n += 16
    SY.dma_start(out=onesf[:, :], in_=ones_in[:, :]).then_inc(cst.h, 16); cst.n += 16
    V.memset(colsum[:, :], 0.0)
    V.memset(totb[:, :], 0.0)
    V.memset(maxbuf[:, :], 0.0)
    V.memset(minbuf[:, :], 0.0)
    V.memset(c1536[:, :], 1536.0)

    xload = {}
    def load_x(tau):
        c = xs_[tau % 2]
        S.dma_start(out=xa[tau % 2][:, :],
                    in_=bass.AP(x_in, tau * P * H, [[H, P], [1, H]]))\
         .then_inc(c.h, 16); c.n += 16
        xload[tau] = (c, c.n)
    load_x(0)
    if TT > 1:
        load_x(1)

    p0a_chunks = []
    CR = 2 * P
    for m, w_t in ((0, wg_in), (1, wu_in)):
        for ci in range((I + CR - 1) // CR):
            r0 = ci * CR
            a = min(CR, I - r0) // P
            p0a_chunks.append(("gu", w_t, r0, a * 512))
    nch_g = (I + CR - 1) // CR
    for hg in range(4):
        for sp in range((I + RAWW - 1) // RAWW):
            c0 = sp * RAWW
            p0a_chunks.append(("d", wd_in, hg * P * I + c0, min(RAWW, I - c0)))
    if phase_limit == 'pa':
        p0a_chunks = []
    assert len(p0a_chunks) <= 160
    p0a_red = {}
    p0a_next = [0]

    def emit_p0a(nmax):
        for _ in range(nmax):
            ci = p0a_next[0]
            if ci >= len(p0a_chunks):
                return
            p0a_next[0] += 1
            kind, w_t, off, width = p0a_chunks[ci]
            if ci >= 2:
                vv.wait(SY, p0a_red[ci - 2])
            if kind == "gu":
                a = width // 512
                dst = bass.AP(raw[ci % 2], 0, [[RAWW, P], [512, a], [1, 512]])
                src = bass.AP(w_t, off, [[512, P], [P * 512, a], [1, 512]])
            else:
                dst = bass.AP(raw[ci % 2], 0, [[RAWW, P], [1, width]])
                src = bass.AP(w_t, off, [[I, P], [1, width]])
            c = pa_[ci % 2]
            SY.dma_start(out=dst, in_=src).then_inc(c.h, 16); c.n += 16
            V.wait_ge(c.h, c.n)
            p0a_red[ci] = V.tensor_reduce(
                colsum[:, ci:ci + 1],
                bass.AP(raw[ci % 2], 0, [[RAWW, P], [1, width]]),
                AX.X, ALU.add, apply_absolute_value=True, _ctr=vv)

    inv_h = 1.0 / float(H)
    yq_copy = {}
    x1_dma = {}

    for tau in range(TT):
        m_mean, m_neg, m_var, m_aeps, m_sig, m_t, m_u, m_s1, m_rstd, m_x = sm
        xt = xa[tau % 2]

        def ln_stats(src):
            xc, xv = xload[tau]
            V.wait_ge(xc.h, xv)
            V.tensor_reduce(m_mean[:, :], src[:, :], AX.X, ALU.add)
            V.tensor_scalar(m_mean[:, :], m_mean[:, :], inv_h, None, ALU.mult)
            v1 = V.tensor_scalar(m_neg[:, :], m_mean[:, :], -1.0, None, ALU.mult, _ctr=vv)
            vv.wait(S, v1)
            S.activation(xcb[:, :], src[:, :], ACT.Identity, bias=m_neg[:, :], scale=1.0)
            s1 = S.activation(f16t[:, :], xcb[:, :], ACT.Square, accum_out=m_var[:, :], _ctr=ss)
            ss.wait(V, s1)
            V.tensor_scalar(m_var[:, :], m_var[:, :], inv_h, None, ALU.mult)
            v2 = V.tensor_scalar(m_aeps[:, :], m_var[:, :], 1e-5, None, ALU.add, _ctr=vv)
            vv.wait(S, v2)
            s2 = S.activation(m_sig[:, :], m_aeps[:, :], ACT.Sqrt, _ctr=ss)
            ss.wait(V, s2)
            V.reciprocal(m_t[:, :], m_sig[:, :])
            V.tensor_tensor(m_t[:, :], m_aeps[:, :], m_t[:, :], ALU.mult)
            V.tensor_tensor(m_u[:, :], m_sig[:, :], m_t[:, :], ALU.add)
            V.tensor_scalar(m_s1[:, :], m_u[:, :], 0.5, None, ALU.mult)
            V.reciprocal(m_rstd[:, :], m_s1[:, :])

        ln_stats(xt)
        V.tensor_scalar(m_x[:, :], m_rstd[:, :], 2.0, None, ALU.mult)
        v4 = V.scalar_tensor_tensor(x1b[:, :], xcb[:, :], m_x[:, :], xt[:, :],
                                    ALU.mult, ALU.add, _ctr=vv)
        vv.wait(S, v4)
        S.dma_start(out=bass.AP(x1_d, tau * P * H, [[H, P], [1, H]]),
                    in_=x1b[:, :]).then_inc(x1st.h, 16); x1st.n += 16
        x1_dma[tau] = x1st.n
        if tau + 2 < TT:
            load_x(tau + 2)

        ln_stats(x1b)
        V.tensor_reduce(m_t[:, :], xcb[:, :], AX.X, ALU.max, apply_absolute_value=True)
        V.tensor_scalar(m_u[:, :], m_t[:, :], m_rstd[:, :], 1e-5, ALU.mult, ALU.max)
        V.tensor_tensor(m_t[:, :], m_u[:, :], m_u[:, :], ALU.mult)
        V.tensor_tensor(c3b[:, tau:tau + 1], m_t[:, :], m_u[:, :], ALU.mult)
        V.reciprocal(m_t[:, :], m_u[:, :])
        V.tensor_scalar(m_t[:, :], m_t[:, :], 127.0, None, ALU.mult)
        v5 = V.tensor_tensor(m_x[:, :], m_t[:, :], m_rstd[:, :], ALU.mult, _ctr=vv)
        vv.wait(S, v5)
        x1st.wait(S, x1_dma[tau])
        S.activation(x1b[:, :], xcb[:, :], ACT.Identity, bias=c1536[:, :], scale=m_x[:, :])
        s3 = S.copy(f16t[:, :], x1b[:, :], _ctr=ss)
        ss.wait(V, s3)
        v6 = V.tensor_scalar(yqb[:, :], f16t[:, :], -1536.0, None, ALU.add, _ctr=vv)

        for b in range(KT // 4):
            if tau == 0 and b == 0:
                cst.wait(Tn, cst.n)
            vv.wait(Tn, v6)
            if (tau * (KT // 4) + b) >= 2:
                key = (tau * (KT // 4) + b) - 2
                vv.wait(Tn, yq_copy[key])
            for j in range(4):
                ins = Tn.matmul(pstY[b % 2][:, j * P:(j + 1) * P],
                                yqb[:, (b * 4 + j) * P:(b * 4 + j + 1) * P],
                                identb[:, :], is_transpose=True,
                                start=(j == 0), stop=(j == 3))
            tb_ = tt.inc(ins)
            tt.wait(V, tb_)
            yq_copy[tau * (KT // 4) + b] = V.tensor_copy(
                bass.AP(Y, (b * 4) * T + tau * P, [[KT * T, P], [T, 4], [1, P]]),
                pstY[b % 2][:, 0:512], _ctr=vv)

        emit_p0a(17)

    emit_p0a(len(p0a_chunks))
    v_Yfin = vv.n

    if phase_limit == 'pa':
        x1st.wait(SY, x1st.n)
        for tau in range(TT):
            SY.dma_start(out=bass.AP(out_d, tau * P * H, [[H, P], [1, H]]),
                         in_=bass.AP(x1_d, tau * P * H, [[H, P], [1, H]]))\
              .then_inc(du_.h, 16); du_.n += 16
        SY.wait_ge(du_.h, du_.n)
        S.wait_ge(du_.h, du_.n)
        stA.close()
        stY.close()
        st.close()
        return nc

    nd0 = 2 * nch_g
    V.tensor_reduce(totb[:, 0:1], colsum[:, 0:nch_g], AX.X, ALU.add)
    V.tensor_reduce(totb[:, 1:2], colsum[:, nch_g:2 * nch_g], AX.X, ALU.add)
    v_tot = V.tensor_reduce(totb[:, 2:3], colsum[:, nd0:p0a_next[0]], AX.X, ALU.add, _ctr=vv)
    vv.wait(Tn, v_tot)
    t_tot = tt.inc(Tn.transpose(ptot[:, :], totb[:, :], identf[:, :]))
    tt.wait(V, t_tot)
    v_sums = V.tensor_reduce(sums4[:, :], ptot[:, :], AX.X, ALU.add, _ctr=vv)
    vv.wait(G, v_sums)
    G.dma_start(out=bass.AP(ar_in, 0, [[4, 1], [1, 4]]),
                in_=bass.AP(sums4, 0, [[1, 4], [1, 1]]),
                allow_slow_non_contiguous=True).then_inc(gd.h, 16); gd.n += 16
    G.wait_ge(gd.h, gd.n)
    arsem = new_coll_sem("ar")
    arsem.inc(G.collective_compute("AllReduce", ALU.add,
                                   replica_groups=[list(range(NC))],
                                   ins=[ar_in.ap().opt()], outs=[ar_out.ap().opt()]))
    G.wait_ge(arsem.h, arsem.n)
    G.dma_start(out=arrow[:, :], in_=ar_out[:, :]).then_inc(gd.h, 16); gd.n += 16
    gd_ar = gd.n

    gd.wait(V, gd_ar)
    V.memset(rowb[:, 0:1], NTOT)
    V.memset(rowb[:, 1:2], 127.0)
    V.reciprocal(rowb[:, 2:3], rowb[:, 0:1])
    V.reciprocal(rowb[:, 3:4], rowb[:, 1:2])
    for m in range(3):
        V.tensor_tensor(meansb[:, m:m + 1], arrow[:, m:m + 1], rowb[:, 2:3], ALU.mult)
        V.tensor_scalar(rowb[:, 8 + m:9 + m], meansb[:, m:m + 1], 1e-5, None, ALU.max)
        V.reciprocal(rowb[:, 4 + m:5 + m], rowb[:, 8 + m:9 + m])
    V.tensor_tensor(rowb[:, 8:9], meansb[:, 0:1], meansb[:, 0:1], ALU.mult)
    V.tensor_tensor(rowb[:, 8:9], rowb[:, 8:9], meansb[:, 1:2], ALU.mult)
    V.tensor_tensor(rowb[:, 9:10], rowb[:, 3:4], rowb[:, 3:4], ALU.mult)
    V.tensor_tensor(rowb[:, 9:10], rowb[:, 9:10], rowb[:, 3:4], ALU.mult)
    V.tensor_tensor(rowb[:, 7:8], rowb[:, 8:9], rowb[:, 9:10], ALU.mult)
    V.tensor_tensor(rowb[:, 8:9], meansb[:, 2:3], rowb[:, 3:4], ALU.mult)
    V.tensor_copy(rowb[:, 0:1], rowb[:, 4:5])
    V.tensor_copy(rowb[:, 1:2], rowb[:, 5:6])
    V.tensor_copy(rowb[:, 2:3], rowb[:, 6:7])
    V.tensor_copy(rowb[:, 3:4], rowb[:, 7:8])
    v_row = V.tensor_copy(rowb[:, 4:5], rowb[:, 8:9], _ctr=vv)
    vv.wait(Tn, v_row)
    t_rep = tt.inc(Tn.matmul(prep8[:, :], onesf[:, :], rowb[:, 0:8], start=True, stop=True))
    tt.wait(V, t_rep)
    v_rep = V.tensor_copy(repscal[:, :], prep8[:, :], _ctr=vv)

    wuse = [0]
    stg_dma = {}
    stg_copy = {}

    def emit_tile_store(src_sb, src_w, src_off, ag_t, dst_off, agwidth, v_src):
        w = wuse[0]; wuse[0] += 1
        vv.wait(Tn, v_src)
        if w >= 2:
            scp.wait(Tn, stg_copy[w - 2])
        for j in range(4):
            ins = Tn.matmul(pstW[w % 2][:, j * P:(j + 1) * P],
                            bass.AP(src_sb, src_off + j * P, [[src_w, P], [1, P]]),
                            identb[:, :], is_transpose=True,
                            start=(j == 0), stop=(j == 3))
        tw = tt.inc(ins)
        tt.wait(S, tw)
        if w >= 4:
            sc, sv = stg_dma[w - 4]
            S.wait_ge(sc.h, sv)
        stg_copy[w] = S.activation(stgW[w % 4][:, :], pstW[w % 2][:, 0:512],
                                   ACT.Identity, _ctr=scp)
        scp.wait(S, stg_copy[w])
        c = stg_[w % 4]
        S.dma_start(out=bass.AP(ag_t, dst_off, [[agwidth, P], [1, 512]]),
                    in_=stgW[w % 4][:, :]).then_inc(c.h, 16); c.n += 16
        stg_dma[w] = (c, c.n)
        return tw

    def publish_group_stores():
        for c in stg_:
            S.wait_ge(c.h, c.n)
        S.activation(sm[1][:, :], sm[0][:, :], ACT.Identity, _ctr=gst)
        return gst.n

    rawuse = [0]
    tern_done = {}
    trans_mark = {}

    p0a_last_red = max(p0a_red.values()) if p0a_red else 0

    def p0b_gu(w_t, scal_col, ag_t, s0, sz):
        agw = sz * 512
        for mt0 in range(s0, s0 + sz):
            ru = rawuse[0]; rawuse[0] += 1
            if ru >= 4:
                SY.wait_ge(vv.h, tern_done[ru - 4])
            elif ru < 2:
                vv.wait(SY, p0a_last_red)
            c = pb_[ru % 4]
            SY.dma_start(out=bass.AP(raw[ru % 4], 0, [[RAWW, P], [1, 512]]),
                         in_=bass.AP(w_t, mt0 * P * 512, [[512, P], [1, 512]]))\
              .then_inc(c.h, 16); c.n += 16
            V.wait_ge(c.h, c.n)
            vv.wait(V, v_rep)
            if ru >= 2 and (ru - 2) in trans_mark:
                tt.wait(V, trans_mark[ru - 2])
            V.tensor_scalar(pb[:, :], bass.AP(raw[ru % 4], 0, [[RAWW, P], [1, 512]]),
                            repscal[:, scal_col:scal_col + 1], 0.5, ALU.mult, ALU.is_gt)
            V.tensor_scalar(nb[:, :], bass.AP(raw[ru % 4], 0, [[RAWW, P], [1, 512]]),
                            repscal[:, scal_col:scal_col + 1], -0.5, ALU.mult, ALU.is_lt)
            tern_done[ru] = V.tensor_tensor(ternb[ru % 2][:, :], pb[:, :], nb[:, :],
                                            ALU.subtract, _ctr=vv)
            trans_mark[ru] = emit_tile_store(
                ternb[ru % 2], 512, 0, ag_t, (mt0 - s0) * 512, agw, tern_done[ru])

    cg_g, cg_u = {}, {}
    for g, (s0, sz) in enumerate(ggrp):
        p0b_gu(wg_in, 0, agg_in[g], s0, sz)
        p0b_gu(wu_in, 1, agu_in[g], s0, sz)
        gp = publish_group_stores()
        gst.wait(G, gp)
        cg_g[g] = new_coll_sem(f"g{g}")
        cg_g[g].inc(G.collective_compute("AllGather", ALU.bypass,
                                         replica_groups=[list(range(NC))],
                                         ins=[agg_in[g].ap().opt()], outs=[agg_out[g].ap().opt()]))
        cg_u[g] = new_coll_sem(f"u{g}")
        cg_u[g].inc(G.collective_compute("AllGather", ALU.bypass,
                                         replica_groups=[list(range(NC))],
                                         ins=[agu_in[g].ap().opt()], outs=[agu_out[g].ap().opt()]))

    tern_markD = {}
    trans_markD = {}
    cg_d = {}
    bidx = [0]
    for gdi, (s0, sz) in enumerate(dgrp):
        agw = sz * 512
        for kt0 in range(s0, s0 + sz, 2):
            cnt = min(2, s0 + sz - kt0)
            b = bidx[0]; bidx[0] += 1
            cpd = pd_[b % 2]
            if b >= 2:
                vv.wait(SY, tern_markD[b - 2])
            for hg in range(4):
                SY.dma_start(out=bass.AP(rawD[b % 2][hg], 0, [[256, P], [1, cnt * P]]),
                             in_=bass.AP(wd_in, hg * P * I + kt0 * P, [[I, P], [1, cnt * P]]))\
                  .then_inc(cpd.h, 16); cpd.n += 16
            cpd.wait(V, cpd.n)
            vv.wait(V, v_rep)
            if b >= 2:
                tt.wait(V, trans_markD[b - 2])
            for hg in range(4):
                w = cnt * P
                V.tensor_scalar(pb[:, 0:w], rawD[b % 2][hg][:, 0:w],
                                repscal[:, 2:3], 0.5, ALU.mult, ALU.is_gt)
                V.tensor_scalar(nb[:, 0:w], rawD[b % 2][hg][:, 0:w],
                                repscal[:, 2:3], -0.5, ALU.mult, ALU.is_lt)
                last = V.tensor_tensor(ternD[b % 2][hg][:, 0:w], pb[:, 0:w], nb[:, 0:w],
                                       ALU.subtract, _ctr=(vv if hg == 3 else None))
            tern_markD[b] = last
            for k in range(cnt):
                w = wuse[0]; wuse[0] += 1
                vv.wait(Tn, tern_markD[b])
                if w >= 2:
                    scp.wait(Tn, stg_copy[w - 2])
                for ml in range(4):
                    ins = Tn.matmul(pstW[w % 2][:, ml * P:(ml + 1) * P],
                                    ternD[b % 2][ml][:, k * P:(k + 1) * P], identb[:, :],
                                    is_transpose=True, start=(ml == 0), stop=(ml == 3))
                tw = tt.inc(ins)
                trans_markD[b] = tw
                tt.wait(S, tw)
                if w >= 4:
                    sc, sv = stg_dma[w - 4]
                    S.wait_ge(sc.h, sv)
                stg_copy[w] = S.activation(stgW[w % 4][:, :], pstW[w % 2][:, 0:512],
                                           ACT.Identity, _ctr=scp)
                scp.wait(S, stg_copy[w])
                c = stg_[w % 4]
                S.dma_start(out=bass.AP(agd_in[gdi], (kt0 - s0 + k) * 512, [[agw, P], [1, 512]]),
                            in_=stgW[w % 4][:, :]).then_inc(c.h, 16); c.n += 16
                stg_dma[w] = (c, c.n)
        gp = publish_group_stores()
        gst.wait(G, gp)
        cg_d[gdi] = new_coll_sem(f"d{gdi}")
        cg_d[gdi].inc(G.collective_compute("AllGather", ALU.bypass,
                                           replica_groups=[list(range(NC))],
                                           ins=[agd_in[gdi].ap().opt()], outs=[agd_out[gdi].ap().opt()]))

    v_p0b = vv.n
    t_p0b = tt.n
    g_p0b = gst.n

    def _dummy_out():
        x1st.wait(SY, x1st.n)
        for tau in range(TT):
            SY.dma_start(out=bass.AP(out_d, tau * P * H, [[H, P], [1, H]]),
                         in_=bass.AP(x1_d, tau * P * H, [[H, P], [1, H]]))\
              .then_inc(du_.h, 16); du_.n += 16
        SY.wait_ge(du_.h, du_.n)
        S.wait_ge(du_.h, du_.n)
        for cs in csem.values():
            G.wait_ge(cs.h, cs.n)
        G.wait_ge(gd.h, gd.n)

    if phase_limit == 'p0':
        _dummy_out()
        if dbg:
            for cs in csem.values():
                SY.wait_ge(cs.h, cs.n)
            for fam, outs, grp in (("dmpg", agg_out, ggrp), ("dmpu", agu_out, ggrp),
                                   ("dmpd", agd_out, dgrp)):
                for g_, (s_, sz_) in enumerate(grp):
                    td = nc.dram_tensor(f"{fam}{g_}", [P * NC, sz_ * 512], BF16,
                                        kind="ExternalOutput")
                    SY.dma_start(out=td[:, :], in_=outs[g_][:, :]).then_inc(du_.h, 16)
                    du_.n += 16
            for fam, ins_, grp in (("dmpig", agg_in, ggrp), ("dmpiu", agu_in, ggrp),
                                   ("dmpid", agd_in, dgrp)):
                for g_, (s_, sz_) in enumerate(grp):
                    td = nc.dram_tensor(f"{fam}{g_}", [P, sz_ * 512], BF16,
                                        kind="ExternalOutput")
                    SY.dma_start(out=td[:, :], in_=ins_[g_][:, :]).then_inc(du_.h, 16)
                    du_.n += 16
            SY.wait_ge(du_.h, du_.n)
        stA.close()
        stY.close()
        st.close()
        return nc

    stA.close()
    stM = ExitStack()
    wbuf = [stM.enter_context(nc.sbuf_tensor(f"wbuf{i}", [P, 2 * KT * P], BF16)) for i in range(2)]
    rgb = [stM.enter_context(nc.sbuf_tensor(f"rgb{i}", [P, T], F32)) for i in range(2)]
    rg2 = [stM.enter_context(nc.sbuf_tensor(f"rg2{i}", [P, T], F32)) for i in range(2)]
    ebuf = [stM.enter_context(nc.sbuf_tensor(f"ebuf{i}", [P, T], F32)) for i in range(2)]
    gps = [stM.enter_context(nc.psum_tensor(f"gps{i}", [P, T], F32)) for i in range(2)]
    ups = [stM.enter_context(nc.psum_tensor(f"ups{i}", [P, T], F32)) for i in range(2)]

    mt_group = {}
    for g, (s0, sz) in enumerate(ggrp):
        for mt in range(s0, s0 + sz):
            mt_group[mt] = g
    wb_dma = {}
    T_MAIN0 = tt.n
    e_dma = {}
    for mt in range(MT):
        g = mt_group[mt]
        s0, sz = ggrp[g]
        SY.wait_ge(cg_g[g].h, cg_g[g].n)
        SY.wait_ge(cg_u[g].h, cg_u[g].n)
        if mt == 0:
            vv.wait(SY, v_p0b)
            tt.wait(SY, t_p0b)
            gst.wait(SY, g_p0b)
        if mt >= 2:
            tt.wait(SY, T_MAIN0 + mt - 1)
        cwb = wb_[mt % 2]
        for ag_t, coff in ((agg_out[g], 0), (agu_out[g], KT * P)):
            SY.dma_start(
                out=bass.AP(wbuf[mt % 2], coff, [[2 * KT * P, P], [512, NC], [1, 512]]),
                in_=bass.AP(ag_t, (mt - s0) * 512, [[sz * 512, P], [P * sz * 512, NC], [1, 512]]))\
              .then_inc(cwb.h, 16); cwb.n += 16
        wb_dma[mt] = (cwb, cwb.n)
        Tn.wait_ge(cwb.h, cwb.n)
        if mt == 0:
            vv.wait(Tn, v_Yfin)
        if mt >= 2:
            se.wait(Tn, mt - 1)
            ve.wait(Tn, mt - 1)
        for kt in range(KT):
            st0, sp0 = (kt == 0), (kt == KT - 1)
            for th in range(NH):
                ins = Tn.matmul(gps[mt % 2][:, th * 512:(th + 1) * 512],
                                wbuf[mt % 2][:, kt * P:(kt + 1) * P],
                                Y[:, kt * T + th * 512:kt * T + (th + 1) * 512],
                                start=st0, stop=sp0)
            for th in range(NH):
                ins = Tn.matmul(ups[mt % 2][:, th * 512:(th + 1) * 512],
                                wbuf[mt % 2][:, KT * P + kt * P:KT * P + (kt + 1) * P],
                                Y[:, kt * T + th * 512:kt * T + (th + 1) * 512],
                                start=st0, stop=sp0)
        tmm = tt.inc(ins)
        tt.wait(S, tmm)
        S.activation(rgb[mt % 2][:, :], gps[mt % 2][:, :], ACT.Relu)
        S.activation(rg2[mt % 2][:, :], rgb[mt % 2][:, :], ACT.Square, _ctr=se)
        se.wait(V, mt + 1)
        if mt >= 2:
            ec, ev = e_dma[mt - 2]
            V.wait_ge(ec.h, ev)
        V.tensor_tensor(ebuf[mt % 2][:, :], rg2[mt % 2][:, :], ups[mt % 2][:, :], ALU.mult)
        V.tensor_tensor(maxbuf2[:, :], ebuf[mt % 2][:, :], maxbuf[:, :], ALU.max)
        V.tensor_tensor(minbuf2[:, :], ebuf[mt % 2][:, :], minbuf[:, :], ALU.min, _ctr=ve)
        maxbuf, maxbuf2 = maxbuf2, maxbuf
        minbuf, minbuf2 = minbuf2, minbuf
        ve.wait(S, mt + 1)
        ce = est_[mt % 2]
        S.dma_start(out=bass.AP(e_d, mt * P * T, [[T, P], [1, T]]),
                    in_=ebuf[mt % 2][:, :]).then_inc(ce.h, 16); ce.n += 16
        e_dma[mt] = (ce, ce.n)

    if phase_limit == 'main':
        _dummy_out()
        stM.close()
        stY.close()
        st.close()
        return nc
    if dbg:
        Y_out = nc.dram_tensor("dbg_Y", [P, KT * T], BF16, kind="ExternalOutput")
        SY.wait_ge(vv.h, v_Yfin)
        SY.dma_start(out=Y_out[:, :], in_=Y[:, :]).then_inc(du_.h, 16); du_.n += 16
    stM.close()
    stY.close()

    psirep = st.enter_context(nc.sbuf_tensor("psirep", [P, T], F32))
    stP = ExitStack()
    pmax = stP.enter_context(nc.psum_tensor("pmax", [P, 512], F32))
    psit = stP.enter_context(nc.psum_tensor("psit", [TT, P], F32))
    prep2 = stP.enter_context(nc.psum_tensor("prep2", [P, 512], F32))
    psiT = stP.enter_context(nc.sbuf_tensor("psiT", [TT, P], F32))
    psirow = stP.enter_context(nc.sbuf_tensor("psirow", [1, T], F32))

    vv.wait(V, v_rep)
    v_gam = V.tensor_scalar(gam[:, :], c3b[:, :], repscal[:, 3:4], None, ALU.mult, _ctr=vv)
    ve.wait(V, MT)
    V.tensor_scalar(minbuf2[:, :], minbuf[:, :], -1.0, None, ALU.mult)
    v_fold = V.tensor_tensor(maxbuf2[:, :], maxbuf[:, :], minbuf2[:, :], ALU.max, _ctr=vv)
    maxbuf = maxbuf2
    last_red = None
    for tau in range(TT):
        vv.wait(Tn, v_fold)
        if last_red is not None:
            vv.wait(Tn, last_red)
        ttr = tt.inc(Tn.transpose(pmax[:, 0:P], maxbuf[:, tau * P:(tau + 1) * P], identf[:, :]))
        tt.wait(V, ttr)
        last_red = V.tensor_reduce(mec[:, tau:tau + 1], pmax[:, 0:P], AX.X, ALU.max, _ctr=vv)
    V.tensor_tensor(tmpa[:, :], gam[:, :], mec[:, :], ALU.mult)
    V.tensor_scalar(tmpa[:, :], tmpa[:, :], 1e-5, None, ALU.max)
    V.tensor_scalar(phic[:, :], tmpa[:, :], repscal[:, 4:5], None, ALU.mult)
    V.reciprocal(tmpb[:, :], tmpa[:, :])
    V.tensor_scalar(tmpb[:, :], tmpb[:, :], 127.0, None, ALU.mult)
    v_psi = V.tensor_tensor(psic[:, :], tmpb[:, :], gam[:, :], ALU.mult, _ctr=vv)
    vv.wait(Tn, v_psi)
    t_pt = tt.inc(Tn.transpose(psit[:, :], psic[:, :], identf[:, :]))
    tt.wait(V, t_pt)
    v_pT = V.tensor_copy(psiT[:, :], psit[:, :], _ctr=vv)
    vv.wait(SY, v_pT)
    SY.dma_start(out=bass.AP(psirow, 0, [[T, 1], [P, TT], [1, P]]),
                 in_=psiT[:, :]).then_inc(pr_s.h, 16); pr_s.n += 16
    v_prep_last = 0
    for j in range(T // 512):
        pr_s.wait(Tn, pr_s.n)
        if v_prep_last:
            vv.wait(Tn, v_prep_last)
        t_r = tt.inc(Tn.matmul(prep2[:, :], onesf[:, :], psirow[:, j * 512:(j + 1) * 512],
                               start=True, stop=True))
        tt.wait(V, t_r)
        v_prep_last = V.tensor_copy(psirep[:, j * 512:(j + 1) * 512], prep2[:, :], _ctr=vv)
    v_psirep = v_prep_last
    stP.close()

    if dbg:
        SY.wait_ge(vv.h, v_psirep)
        for nm, sb, rr, cc_ in (("dbgE_maxbuf", maxbuf, P, T), ("dbgE_mec", mec, P, TT),
                                ("dbgE_gam", gam, P, TT), ("dbgE_phic", phic, P, TT),
                                ("dbgE_psic", psic, P, TT)):
            tdb = nc.dram_tensor(nm, [rr, cc_], F32, kind="ExternalOutput")
            SY.dma_start(out=tdb[:, :], in_=sb[:, :]).then_inc(du_.h, 16); du_.n += 16

    if phase_limit == 'psi':
        vv.wait(SY, v_psirep)
        _dummy_out()
        st.close()
        return nc

    stD = ExitStack()
    Hq = stD.enter_context(nc.sbuf_tensor("Hq", [P, MT * 512], BF16))
    etile = [stD.enter_context(nc.sbuf_tensor(f"etile{i}", [P, 512], F32)) for i in range(2)]
    vb2 = [stD.enter_context(nc.sbuf_tensor(f"vb2_{i}", [P, 512], F32)) for i in range(2)]
    fb2 = [stD.enter_context(nc.sbuf_tensor(f"fb2_{i}", [P, 512], F16)) for i in range(2)]
    wdbW = max(sz for _, sz in dgrp) * 512
    wdb = [stD.enter_context(nc.sbuf_tensor(f"wdb{i}", [P, wdbW], BF16)) for i in range(2)]
    x1s = [[stD.enter_context(nc.sbuf_tensor(f"x1s{i}_{t}", [P, 512], F32)) for t in range(TG)]
           for i in range(2)]
    outb = [stD.enter_context(nc.sbuf_tensor(f"outb{i}", [P, TG * 512], F32)) for i in range(2)]
    dps = [stD.enter_context(nc.psum_tensor(f"dps{i}", [P, TG * 512], F32)) for i in range(2)]

    est_[0].wait(SY, est_[0].n)
    est_[1].wait(SY, est_[1].n)
    x1st.wait(SY, x1st.n)
    n_round = 0
    out_dma = {}
    sq_mark = {}
    t3_blocks = []
    for g2 in range(NTG):
        for kt in range(MT):
            if kt >= 2:
                vq.wait(SY, g2 * MT + kt - 1)
            cet = et_[kt % 2]
            SY.dma_start(out=etile[kt % 2][:, :],
                         in_=bass.AP(e_d, kt * P * T + g2 * TG * P, [[T, P], [1, 512]]))\
              .then_inc(cet.h, 16); cet.n += 16
            if g2 == 1:
                for hi, cnt_ in t3_blocks:
                    if kt <= hi:
                        t4.wait(V, cnt_)
                        break
            V.wait_ge(cet.h, cet.n)
            if g2 == 0 and kt == 0:
                vv.wait(V, v_psirep)
            kq = g2 * MT + kt
            if kq >= 2:
                ss.wait(V, sq_mark[kq - 2])
            vm = V.tensor_tensor(vb2[kt % 2][:, :], etile[kt % 2][:, :],
                                 psirep[:, g2 * TG * P:(g2 * TG + TG) * P],
                                 ALU.mult, _ctr=vv)
            vv.wait(S, vm)
            if kq >= 2:
                vq.wait(S, kq - 1)
            sq_mark[kq] = S.activation(fb2[kt % 2][:, :], vb2[kt % 2][:, :],
                                       ACT.Identity, bias=c1536[:, :], scale=1.0,
                                       _ctr=ss)
            ss.wait(V, sq_mark[kq])
            V.tensor_scalar(Hq[:, kt * 512:(kt + 1) * 512], fb2[kt % 2][:, :],
                            -1536.0, None, ALU.add, _ctr=vq)
        for hq in range(HQ):
            par = n_round % 2
            if n_round >= 2:
                vd.wait(SY, n_round - 1)
            cx1 = xd_[par]
            for tl in range(TG):
                SY.dma_start(out=x1s[par][tl][:, :],
                             in_=bass.AP(x1_d, (g2 * TG + tl) * P * H + hq * 512,
                                         [[H, P], [1, 512]]))\
                  .then_inc(cx1.h, 16); cx1.n += 16
            x1_cnt = cx1.n
            if n_round >= 2:
                vd.wait(Tn, n_round - 1)
            for gdi, (s0, sz) in enumerate(dgrp):
                seg = n_round * NGD + gdi
                wpar = seg % 2
                SY.wait_ge(cg_d[gdi].h, cg_d[gdi].n)
                if seg >= 2:
                    t4.wait(SY, seg - 1)
                cwd = wd_[wpar]
                SY.dma_start(out=bass.AP(wdb[wpar], 0, [[wdbW, P], [1, sz * 512]]),
                             in_=bass.AP(agd_out[gdi], hq * P * sz * 512,
                                         [[sz * 512, P], [1, sz * 512]]))\
                  .then_inc(cwd.h, 16); cwd.n += 16
                Tn.wait_ge(cwd.h, cwd.n)
                for k in range(sz):
                    kt = s0 + k
                    vq.wait(Tn, g2 * MT + kt + 1)
                    for tl in range(TG):
                        ins = Tn.matmul(dps[par][:, tl * 512:(tl + 1) * 512],
                                        Hq[:, kt * 512 + tl * P:kt * 512 + (tl + 1) * P],
                                        wdb[wpar][:, k * 512:(k + 1) * 512],
                                        start=(kt == 0), stop=(kt == MT - 1))
                t4.inc(ins)
                if g2 == 0 and hq == HQ - 1:
                    t3_blocks.append((s0 + sz - 1, t4.n))
            t4.wait(V, (n_round + 1) * NGD)
            V.wait_ge(cx1.h, x1_cnt)
            if n_round >= 2:
                oc, ov = out_dma[n_round - 2]
                V.wait_ge(oc.h, ov)
            for tl in range(TG):
                v_dr = V.scalar_tensor_tensor(
                    outb[par][:, tl * 512:(tl + 1) * 512],
                    dps[par][:, tl * 512:(tl + 1) * 512],
                    phic[:, g2 * TG + tl:g2 * TG + tl + 1],
                    x1s[par][tl][:, :], ALU.mult, ALU.add,
                    _ctr=(vd if tl == TG - 1 else None))
            vd.wait(S, v_dr)
            cos = os_[par]
            for tl in range(TG):
                S.dma_start(out=bass.AP(out_d, (g2 * TG + tl) * P * H + hq * 512,
                                        [[H, P], [1, 512]]),
                            in_=outb[par][:, tl * 512:(tl + 1) * 512])\
                  .then_inc(cos.h, 16); cos.n += 16
            out_dma[n_round] = (cos, cos.n)
            n_round += 1

    stD.close()

    if dbg:
        SY.wait_ge(vd.h, n_round)
        for nm, sb, rr, cc_ in (("dbg_maxbuf", maxbuf, P, T), ("dbg_mec", mec, P, TT),
                                ("dbg_gam", gam, P, TT), ("dbg_phic", phic, P, TT),
                                ("dbg_psic", psic, P, TT), ("dbg_rep", repscal, P, 8),
                                ("dbg_psirep", psirep, P, T)):
            tdb = nc.dram_tensor(nm, [rr, cc_], F32, kind="ExternalOutput")
            SY.dma_start(out=tdb[:, :], in_=sb[:, :]).then_inc(du_.h, 16); du_.n += 16

    S.wait_ge(os_[0].h, os_[0].n)
    S.wait_ge(os_[1].h, os_[1].n)
    SY.wait_ge(os_[0].h, os_[0].n)
    SY.wait_ge(os_[1].h, os_[1].n)
    if du_.n:
        SY.wait_ge(du_.h, du_.n)
    for cs in csem.values():
        G.wait_ge(cs.h, cs.n)
    G.wait_ge(gd.h, gd.n)

    st.close()
    return nc


def kernel(L0_attn_x, L0_ffn_w_gate, L0_ffn_w_up, L0_ffn_w_down):
    x = np.asarray(L0_attn_x, dtype=np.float32)
    wg = np.asarray(L0_ffn_w_gate, dtype=np.float32)
    wu = np.asarray(L0_ffn_w_up, dtype=np.float32)
    wd = np.asarray(L0_ffn_w_down, dtype=np.float32)
    B, Seq, Hm = x.shape
    I = wg.shape[0]
    T = (B * Seq) // NC
    nc = build_nc(I=I, T=T, NG=8, NGD=8)
    identb = np.eye(P, dtype=ml_dtypes.bfloat16)
    identf = np.eye(P, dtype=np.float32)
    ones = np.ones((1, P), np.float32)
    x2 = x.reshape(B * Seq, Hm)
    hs = Hm // NC
    in_maps = []
    for r in range(NC):
        in_maps.append({
            "x_in": np.ascontiguousarray(x2[r * T:(r + 1) * T]),
            "wg_in": np.ascontiguousarray(wg[:, r * hs:(r + 1) * hs]),
            "wu_in": np.ascontiguousarray(wu[:, r * hs:(r + 1) * hs]),
            "wd_in": np.ascontiguousarray(wd[r * hs:(r + 1) * hs, :]),
            "identb_in": identb, "identf_in": identf, "ones_in": ones,
        })
    res = run_bass_kernel_spmd(nc, in_maps, core_ids=list(range(NC)))
    out = np.concatenate([np.asarray(res.results[r]["out"]) for r in range(NC)], axis=0)
    return out.reshape(B, Seq, Hm).astype(np.float32)
